# version 01
# baseline (speedup 1.0000x reference)
import numpy as np
from contextlib import ExitStack

import concourse.bass as bass
import concourse.mybir as mybir
from concourse.bass_utils import run_bass_kernel_spmd

F32 = mybir.dt.float32
AF = mybir.ActivationFunctionType
ALU = mybir.AluOpType
AX = mybir.AxisListType.X

T = 512
DM, DCH = 1024, 8
DI, ICH = 2048, 16
NS = 16
NE, HD, HCH = 8, 2048, 16
EPS = 1e-5



class B:

    ENGS = ("tensor", "vector", "scalar", "gpsimd", "sync")

    def __init__(self):
        self.q = {e: [] for e in self.ENGS}
        self.cnt = {}

    def sem(self, *names):
        for n in names:
            self.cnt[n] = 0

    def do(self, eng, fn, sig=None, inc=1):
        if sig is None:
            self.q[eng].append(lambda e, S, fn=fn: fn(e))
            return None
        self.cnt[sig] += inc
        tok = self.cnt[sig]

        if eng in ("vector", "scalar") and inc == 1:
            def g(e, S, fn=fn, sig=sig, inc=inc):
                fn(e)
                e.drain()
                e.nop().then_inc(S[sig], inc)
        else:
            def g(e, S, fn=fn, sig=sig, inc=inc):
                fn(e).then_inc(S[sig], inc)

        self.q[eng].append(g)
        return tok

    def w(self, eng, sem, tok):
        if tok:
            self.q[eng].append(
                lambda e, S, sem=sem, tok=tok: e.wait_ge(S[sem], tok))

    def dma(self, eng, sem, out, in_):
        return self.do(eng, lambda e, o=out, i=in_: e.dma_start(out=o, in_=i),
                       sig=sem, inc=16)




def _stage_weights(ins):
    f = np.float32

    def c(a):
        return np.ascontiguousarray(a)

    return dict(
        wip=c(np.asarray(ins["L0_attn_in_proj_w"], f)
              .reshape(32, 128, 8, 128).transpose(0, 2, 3, 1)),
        wxp=c(np.asarray(ins["L0_attn_x_proj_w"], f)
              .reshape(96, 16, 128).transpose(1, 2, 0)),
        wdt=c(np.asarray(ins["L0_attn_dt_proj_w"], f)
              .reshape(16, 128, 64).transpose(0, 2, 1)),
        wop=c(np.asarray(ins["L0_attn_out_proj_w"], f)
              .reshape(1024, 16, 128).transpose(1, 2, 0)),
        wglu=c(np.asarray(ins["L0_attn_glu_w"], f)
               .reshape(2048, 8, 128).transpose(1, 2, 0)),
        wg=c(np.asarray(ins["L0_ffn_Wg"], f).reshape(8, 128, 8)),
        w1=c(np.asarray(ins["L0_ffn_W1"], f).reshape(8, 8, 128, 2048)),
        w2=c(np.asarray(ins["L0_ffn_W2"], f).reshape(8, 16, 128, 1024)),
        cw=c(np.asarray(ins["L0_attn_conv_w"], f)
             .reshape(16, 128, 4).transpose(1, 0, 2).reshape(128, 64)),
        cb=c(np.asarray(ins["L0_attn_conv_b"], f).reshape(16, 128).T),
        alog=c(np.asarray(ins["L0_attn_A_log"], f)
               .reshape(16, 128, 16).transpose(1, 0, 2).reshape(128, 256)),
        dvec=c(np.asarray(ins["L0_attn_D"], f).reshape(16, 128).T),
        dtb=c(np.asarray(ins["L0_attn_dt_proj_b"], f).reshape(16, 128).T),
        b1=c(np.asarray(ins["L0_ffn_b1"], f)
             .reshape(8, 16, 128).transpose(2, 0, 1).reshape(128, 128)),
        b2mm=c(np.asarray(ins["L0_ffn_b2"], f)),
        lng=c(np.asarray(ins["L0_attn_ln_g"], f).reshape(8, 128).T),
        lnb=c(np.asarray(ins["L0_attn_ln_b"], f).reshape(8, 128).T),
        glub=c(np.asarray(ins["L0_attn_glu_b"], f).reshape(16, 128).T),
        ident=np.eye(128, dtype=f),
        selb=c(np.eye(96, dtype=f)[:, 64:80]),
        selc=c(np.eye(96, dtype=f)[:, 80:96]),
        d8=c(np.eye(8, dtype=f).reshape(1, 64)),
    )


def _stage_x(x, wtiles, rtiles):
    f = np.float32
    x = np.asarray(x, f)
    W, R = wtiles * T, rtiles * T
    spans = []
    for core in range(8):
        seq, s0 = core // 4, (core % 4) * 2048
        w0 = max(0, s0 - W)
        win = np.concatenate(
            [np.zeros((W - (s0 - w0), DM), f), x[seq, w0:s0, :]], axis=0)
        span = np.concatenate([win, x[seq, s0:s0 + R, :]], axis=0)
        spans.append(np.ascontiguousarray(span.T))
    return spans




def build_program(wtiles=5, rtiles=4, stage="full"):
    ntiles = wtiles + rtiles
    span = ntiles * T
    real = rtiles * T

    nc = bass.Bass("TRN2", target_bir_lowering=False,
                   detect_race_conditions=False)
    dr = {}
    for name, shape in [
            ("xs", [DM, span]), ("wip", [32, 8, 128, 128]),
            ("wxp", [16, 128, 96]), ("wdt", [16, 64, 128]),
            ("wop", [16, 128, 1024]), ("wglu", [8, 128, 2048]),
            ("wg", [8, 128, 8]), ("w1", [8, 8, 128, 2048]),
            ("w2", [8, 16, 128, 1024]), ("cw", [128, 64]), ("cb", [128, 16]),
            ("alog", [128, 256]), ("dvec", [128, 16]), ("dtb", [128, 16]),
            ("b1", [128, 128]), ("b2mm", [8, 1024]),
            ("lng", [128, 8]), ("lnb", [128, 8]), ("glub", [128, 16]),
            ("ident", [128, 128]), ("d8", [1, 64]),
            ("selb", [96, 16]), ("selc", [96, 16])]:
        dr[name] = nc.dram_tensor(name, shape, F32, kind="ExternalInput").ap()
    dr["zscr"] = nc.dram_tensor("zscr", [rtiles, ICH, 128, T], F32,
                                kind="Internal").ap()
    dr["yout"] = nc.dram_tensor("yout", [DM, real], F32,
                                kind="ExternalOutput").ap()
    import os as _os
    _tap = _os.environ.get("KTAP", "")
    if _tap:
        dr["dbg"] = nc.dram_tensor("dbg", [128, 16 * T], F32,
                                   kind="ExternalOutput").ap()

    p = B()
    p.sem("sx", "wq1a", "wq1b", "wq2a", "wq2b", "wipa", "wipb",
          "wopa", "wopb", "zoa", "zob", "zia", "zib", "out",
          "te", "ve", "se", "ge")

    es = ExitStack()
    with es:
        sb = {}

        def alloc(name, shape):
            sb[name] = es.enter_context(nc.sbuf_tensor(name, shape, F32))
            return sb[name]

        x1res = alloc("x1res", [128, DCH * real])
        hstate = alloc("hstate", [128, 256])
        atile = alloc("atile", [128, 256])
        xprojw = alloc("xprojw", [128, 16 * 96])
        dtprojw = alloc("dtprojw", [64, 16 * 128])
        wgw = alloc("wgw", [128, 64])
        identt = alloc("identt", [128, 128])
        stats = alloc("stats", [128, 2 * T])
        scorest = alloc("scorest", [128, 2 * T])
        cwt = alloc("cwt", [128, 64])
        cbt = alloc("cbt", [128, 16])
        dvt = alloc("dvt", [128, 16])
        dtbt = alloc("dtbt", [128, 16])
        b1t = alloc("b1t", [128, 128])
        b2mmt = alloc("b2mmt", [8, 1024])
        lngt = alloc("lngt", [128, 8])
        lnbt = alloc("lnbt", [128, 8])
        glubt = alloc("glubt", [128, 16])
        selone = alloc("selone", [8, 128])
        d8t = alloc("d8t", [1, 64])
        onesc = alloc("onesc", [128, 1])
        ones1 = alloc("ones1", [1, 128])
        xdbl = alloc("xdbl", [128, 512])
        carry = alloc("carry", [128, 48])
        selbt = alloc("selbt", [96, 16])
        selct = alloc("selct", [96, 16])
        AR = alloc("arena", [128, 26192])

        def ar(off, size):
            assert off + size <= 26192
            return AR[:, off:off + size]

        def xn_ch(i, c):
            return AR[:, i * 4096 + c * T: i * 4096 + (c + 1) * T]

        X1WF = ar(4096, 4096)
        HT = lambda j: ar(8240 + j * T, T)
        LOGT = AR[0:8, 8240:8240 + T]
        WQ1 = lambda par: ar(16432 + par * 2048, 2048)
        WQ2 = lambda par: ar(20528 + par * 1024, 1024)
        SSRC = ar(22576, T)
        TMP = ar(23088, T)
        TMP2 = ar(23600, T)
        SROW = lambda i: AR[0:1, 24112 + i * T: 24112 + (i + 1) * T]
        RS = lambda i: AR[:, 26160 + i: 26161 + i]

        XPC = lambda c: AR[:, c * 515:(c + 1) * 515]
        DT_ = lambda c: AR[:, c * 515: c * 515 + 512]
        XC = lambda c: ar(8240 + c * T, T)
        X1WM = ar(8240, 4096)
        XN3 = lambda c: ar(16432 + c * T, T)
        WSA = lambda par: ar(20528 + par * 1024, 1024)
        ZST = lambda par: ar(12336 + par * T, T)
        SC = lambda i: ar(16432 + i * T, T)
        YF = lambda c: ar(17968 + c * T, T)
        WSO = lambda par: ar(par * 1024, 1024)
        Y2A = ar(2048, T)
        Y2B = ar(2560, T)
        ZSo = ar(3072, T)
        ZINo = ar(3584, T)
        MACC = lambda m: ar(4096 + m * T, T)
        MT = lambda m: ar(8240 + m * T, T)
        G1 = lambda ob: ar(12336 + ob * T, T)
        SG = lambda ob: ar(16432 + ob * T, T)
        XN2S = ar(20528, T)

        psum = [es.enter_context(nc.psum_tensor(f"pb{i}", [128, T], F32))
                for i in range(8)]

        bankw = [None] * 8

        def bank_wait(eng, b):
            if bankw[b] is not None:
                sem, tok = bankw[b]
                p.w(eng, sem, tok)
                bankw[b] = None

        def mm(lhsT, rhs, bank, start, stop, sig=None, mrows=128):
            return p.do("tensor",
                        lambda e, o=psum[bank][0:mrows, 0:T], l=lhsT, r=rhs,
                        st=start, sp=stop:
                        e.matmul(o, l, r, start=st, stop=sp), sig=sig)

        def recip_safe(dst, srcap, k1, k2):
            vt = p.do("vector", lambda e: e.engine_nop(), sig="ve")
            p.w("scalar", "ve", vt)
            p.do("scalar", lambda e: e.activation(RS(k1), srcap, AF.Square))
            p.do("scalar", lambda e: e.activation(RS(k2), RS(k1), AF.Ln))
            st = p.do("scalar", lambda e: e.activation(
                RS(k1), RS(k2), AF.Exp, scale=-1.0), sig="se")
            p.w("vector", "se", st)
            p.do("vector", lambda e: e.tensor_tensor(
                dst, srcap, RS(k1), ALU.mult))
            p.do("vector", lambda e: e.drain())

        def bcast(row, bank, sig=None):
            return p.do("tensor", lambda e, r=row, b=bank:
                        e.matmul(psum[b][:, 0:T], ones1[:], r,
                                 start=True, stop=True), sig=sig)

        p.do("vector", lambda e: e.memset(AR[:, 0:13096], 0.0))
        for gi in range(30):
            sl = (slice(13096, 26192) if gi % 2 == 0 else slice(0, 13096))
            s2 = (slice(0, 13096) if gi % 2 == 0 else slice(13096, 26192))
            p.do("vector", lambda e, sl=sl, s2=s2: e.tensor_copy(
                AR[:, sl], AR[:, s2]))
        guard = p.do("vector", lambda e: e.engine_nop(), sig="ve")
        p.w("tensor", "ve", guard)
        p.w("scalar", "ve", guard)
        p.w("gpsimd", "ve", guard)
        p.do("vector", lambda e: e.memset(onesc[:], 1.0))
        p.do("vector", lambda e: e.memset(ones1[:], 1.0))
        p.do("vector", lambda e: e.memset(hstate[:], 0.0))
        p.do("vector", lambda e: e.memset(carry[:], 0.0))
        p.do("vector", lambda e: e.drain())

        for name, dst, resh in [
                ("wxp", xprojw, ("k p j -> p k j", "p (k j) -> p k j", 16)),
                ("wdt", dtprojw, ("m p q -> p m q", "p (m q) -> p m q", 16)),
                ("wg", wgw, ("k p j -> p k j", "p (k j) -> p k j", 8))]:
            s_pat, d_pat, kk = resh
            p.dma("sync", "sx", dst[:].rearrange(d_pat, **(
                {"k": kk} if "k" in d_pat.split("->")[1] else {"m": kk})),
                dr[name].rearrange(s_pat))
        for name, dst in [("ident", identt), ("cw", cwt), ("cb", cbt),
                          ("dvec", dvt), ("dtb", dtbt), ("b1", b1t),
                          ("b2mm", b2mmt), ("lng", lngt), ("lnb", lnbt),
                          ("glub", glubt), ("d8", d8t),
                          ("selb", selbt), ("selc", selct)]:
            p.dma("sync", "sx", dst[:], dr[name][:, :])
        p.w("sync", "ve", p.cnt["ve"])
        alog_tok = p.dma("sync", "sx", atile[:], dr["alog"][:, :])
        p.w("scalar", "sx", alog_tok)
        p.do("scalar", lambda e: e.activation(atile[:], atile[:], AF.Exp))
        se_a = p.do("scalar", lambda e: e.mul(atile[:], atile[:], -1.0),
                    sig="se")
        consts_tok = p.cnt["sx"]

        def ln_stats(xch, nch, mu_row, r_row, var_row):
            bank_wait("tensor", 7)
            p.w("scalar", "ve", p.cnt["ve"])
            prev_mm = None
            for c in range(nch):
                if prev_mm is not None:
                    p.w("scalar", "te", prev_mm)
                sq = p.do("scalar", lambda e, c=c: e.activation(
                    TMP, xch(c), AF.Square), sig="se")
                p.do("tensor", lambda e, c=c, n=nch: e.matmul(
                    psum[7][0:1, 0:T], onesc[:], xch(c),
                    start=(c == 0), stop=(c == n - 1)))
                p.w("tensor", "se", sq)
                prev_mm = p.do("tensor", lambda e, c=c, n=nch: e.matmul(
                    psum[7][32:33, 0:T], onesc[:], TMP[:],
                    start=(c == 0), stop=(c == n - 1)), sig="te")
            p.w("vector", "te", prev_mm)
            nf = float(nch * 128)
            p.do("vector", lambda e: e.tensor_scalar_mul(
                mu_row, psum[7][0:1, 0:T], 1.0 / nf))
            p.do("vector", lambda e: e.tensor_scalar_mul(
                var_row, psum[7][32:33, 0:T], 1.0 / nf))
            p.do("vector", lambda e: e.tensor_tensor(
                TMP[0:1, 0:T], mu_row, mu_row, ALU.mult))
            p.do("vector", lambda e: e.drain())
            p.do("vector", lambda e: e.tensor_tensor(
                var_row, var_row, TMP[0:1, 0:T], ALU.subtract))
            p.do("vector", lambda e: e.drain())
            ve_tok = p.do("vector", lambda e: e.tensor_scalar_add(
                TMP[0:1, 0:T], var_row, EPS), sig="ve")
            p.w("scalar", "ve", ve_tok)
            p.do("scalar", lambda e: e.activation(
                TMP2[0:1, 0:T], TMP[0:1, 0:T], AF.Ln))
            se_tok = p.do("scalar", lambda e: e.activation(
                r_row, TMP2[0:1, 0:T], AF.Exp, scale=-0.5), sig="se")
            return ve_tok, se_tok

        def router(xnch, tile_col):
            bank_wait("tensor", 5)
            bank_wait("tensor", 6)
            lt = None
            for k in range(DCH):
                lt = mm(wgw[:, k * 8:(k + 1) * 8], xnch(k), 5,
                        k == 0, k == DCH - 1,
                        sig="te" if k == DCH - 1 else None, mrows=8)
            p.w("vector", "te", lt)
            vt = p.do("vector", lambda e: e.tensor_copy(
                LOGT, psum[5][0:8, 0:T]), sig="ve")
            PT = TMP[:, 0:8]
            EX = TMP[:, 16:24]
            SG1 = TMP[:, 32:40]
            EQ1 = TMP[:, 48:56]
            PM = TMP[:, 64:72]
            SG2 = TMP[:, 80:88]
            EQ2 = TMP[:, 96:104]
            T4 = TMP[:, 112:116]
            T2 = TMP[:, 128:130]
            SC1 = TMP[:, 144:152]
            SC2 = TMP[:, 160:168]
            SCR = TMP[:, 176:184]

            def pv(fn):
                p.do("vector", fn)
                p.do("vector", lambda e: e.drain())

            def tree8(srcfn, dst, op):
                pv(lambda e, s=srcfn, op=op: e.tensor_tensor(
                    T4, s()[:, 0:4], s()[:, 4:8], op))
                pv(lambda e, op=op: e.tensor_tensor(
                    T2, T4[:, 0:2], T4[:, 2:4], op))
                pv(lambda e, d=dst, op=op: e.tensor_tensor(
                    d, T2[:, 0:1], T2[:, 1:2], op))

            ptv = vt
            for tg in range(4):
                sl = slice(tg * 128, (tg + 1) * 128)
                p.w("tensor", "ve", ptv)
                tt = p.do("tensor", lambda e, sl=sl: e.transpose(
                    psum[6][:, 0:8], LOGT[:, sl], identt[0:8, 0:8]),
                    sig="te")
                p.w("vector", "te", tt)
                ptv = p.do("vector", lambda e: e.tensor_copy(
                    PT, psum[6][:, 0:8]), sig="ve")
                tree8(lambda: PT, RS(0), ALU.max)
                v2 = p.do("vector", lambda e: e.tensor_scalar_mul(
                    RS(1), RS(0), -1.0), sig="ve")
                if _tap == "rs1" and tg == 3 and tile_col == T:
                    p.w("gpsimd", "ve", v2)
                    p.dma("gpsimd", "zob", dr["dbg"][0:128, 0:192],
                          TMP[:, 0:192])
                    zt = p.dma("gpsimd", "zob", dr["dbg"][0:128, 192:224],
                               AR[:, 26160:26160 + 32])
                    p.w("vector", "zob", zt)
                p.w("scalar", "ve", v2)
                st = p.do("scalar", lambda e: e.activation(
                    EX, PT, AF.Exp, bias=RS(1)), sig="se")
                p.w("vector", "se", st)
                tree8(lambda: EX, RS(4), ALU.max)
                v2 = p.do("vector", lambda e: e.tensor_scalar_mul(
                    RS(5), RS(4), -1.0), sig="ve")
                p.w("scalar", "ve", v2)
                st = p.do("scalar", lambda e: e.activation(
                    SG1, EX, AF.Sign, bias=RS(5)), sig="se")
                p.w("vector", "se", st)
                pv(lambda e: e.tensor_scalar_add(EQ1, SG1, 1.0))
                tree8(lambda: EQ1, RS(6), ALU.add)
                pv(lambda e: e.scalar_tensor_tensor(
                    PM, EQ1, -2.0, EX, ALU.mult, ALU.add))
                tree8(lambda: PM, RS(7), ALU.max)
                v2 = p.do("vector", lambda e: e.tensor_scalar_mul(
                    RS(8), RS(7), -1.0), sig="ve")
                p.w("scalar", "ve", v2)
                st = p.do("scalar", lambda e: e.activation(
                    SG2, PM, AF.Sign, bias=RS(8)), sig="se")
                p.w("vector", "se", st)
                pv(lambda e: e.tensor_scalar_add(EQ2, SG2, 1.0))
                tree8(lambda: EQ2, RS(9), ALU.add)
                pv(lambda e: e.tensor_scalar_add(RS(10), RS(6), -1.0))
                pv(lambda e: e.tensor_scalar_min(RS(11), RS(10), 1.0))
                pv(lambda e: e.tensor_tensor(
                    RS(26), RS(4), RS(7), ALU.add))
                pv(lambda e: e.tensor_scalar_add(RS(12), RS(26), 1e-15))
                recip_safe(RS(13), RS(12), 22, 23)
                recip_safe(RS(14), RS(6), 22, 23)
                recip_safe(RS(15), RS(9), 22, 23)
                pv(lambda e: e.tensor_scalar(
                    RS(16), RS(11), -1.0, 1.0, ALU.mult, ALU.add))
                pv(lambda e: e.tensor_tensor(RS(17), RS(4), RS(13), ALU.mult))
                pv(lambda e: e.tensor_tensor(
                    RS(18), RS(17), RS(16), ALU.mult))
                pv(lambda e: e.tensor_tensor(
                    RS(19), RS(11), RS(14), ALU.mult))
                pv(lambda e: e.tensor_tensor(
                    RS(2), RS(18), RS(19), ALU.add))
                pv(lambda e: e.tensor_tensor(RS(20), RS(7), RS(13), ALU.mult))
                pv(lambda e: e.tensor_tensor(
                    RS(21), RS(20), RS(16), ALU.mult))
                pv(lambda e: e.tensor_tensor(
                    RS(3), RS(21), RS(15), ALU.mult))
                pv(lambda e: e.tensor_scalar_mul(RS(24), RS(2), 0.5))
                v2 = p.do("vector", lambda e: e.tensor_scalar_mul(
                    RS(25), RS(3), 0.5), sig="ve")
                pv(lambda e: e.tensor_scalar_mul(SC1, EQ1, RS(24)))
                pv(lambda e: e.tensor_scalar_mul(SC2, EQ2, RS(25)))
                v2 = p.do("vector", lambda e: e.tensor_tensor(
                    SCR, SC1, SC2, ALU.add), sig="ve")
                p.w("tensor", "ve", v2)
                bank_wait("tensor", 7)
                tt = p.do("tensor", lambda e: e.transpose(
                    psum[7][0:8, 0:128], SCR, identt[:]), sig="te")
                p.w("vector", "te", tt)
                scv = p.do("vector", lambda e, tg=tg, tc=tile_col:
                           e.tensor_copy(
                               scorest[0:8, tc + tg * 128:
                                       tc + (tg + 1) * 128],
                               psum[7][0:8, 0:128]), sig="ve")
                bankw[7] = ("ve", scv)
            p.w("scalar", "te", tt)
            p.w("vector", "te", tt)
            if _tap == "scores":
                p.w("gpsimd", "ve", p.cnt["ve"])
                p.dma("gpsimd", "zob", dr["dbg"][0:8, 0:2 * T], scorest[0:8, :])
                p.dma("gpsimd", "zob", dr["dbg"][0:128, 2 * T:2 * T + 192],
                      TMP[:, 0:192])
                p.dma("gpsimd", "zob",
                      dr["dbg"][0:128, 2 * T + 192:2 * T + 224],
                      AR[:, 26160:26160 + 32])
            return p.do("vector", lambda e: e.engine_nop(), sig="ve")

        wpar = {"q1": 0, "q2": 0, "ip": 0, "op": 0}
        wpend = {}

        def stream(tag, sems, buf_of, src_of):
            par = wpar[tag]
            sem = sems[par]
            key = (tag, par)
            if key in wpend:
                csem, ctok = wpend[key]
                p.w("sync", csem, ctok)
            tok = p.dma("sync", sem, buf_of(par), src_of())
            wpar[tag] ^= 1
            return par, sem, tok, key

        def fence_sync(gpsimd=False):
            for s in ("te", "ve", "se"):
                p.w("sync", s, p.cnt[s])
                if gpsimd:
                    p.w("gpsimd", s, p.cnt[s])

        def emit_ffn(tiles, x1_of):
            fence_sync()
            routers_only = _os.environ.get("KFFN") == "ln"
            n_experts = int(_os.environ.get("KNE", str(NE)))
            for i, ti in enumerate(tiles):
                xch = x1_of(ti)
                vtk, stk = ln_stats(xch, DCH, SROW(0), SROW(2), SROW(1))
                p.w("tensor", "ve", vtk)
                p.w("tensor", "se", stk)
                bcast(SROW(0), 6)
                brs = bcast(SROW(2), 7, sig="te")
                p.w("vector", "te", brs)
                lastv = None
                for c in range(DCH):
                    p.do("vector", lambda e, i=i, c=c, xch=xch:
                         e.tensor_tensor(xn_ch(i, c), xch(c),
                                         psum[6][:, 0:T], ALU.subtract))
                    lastv = p.do("vector", lambda e, i=i, c=c:
                                 e.tensor_tensor(xn_ch(i, c), xn_ch(i, c),
                                                 psum[7][:, 0:T], ALU.mult),
                                 sig="ve" if c == DCH - 1 else None)
                p.w("tensor", "ve", lastv)
                router(lambda c, i=i: xn_ch(i, c), i * T)
            prev_sr = None
            for e_i in range(0 if routers_only else n_experts):
                bank_wait("tensor", 5)
                bt = p.do("tensor", lambda e, e_i=e_i: e.matmul(
                    psum[5][0:8, 0:128], d8t[0:1, e_i * 8:e_i * 8 + 8],
                    ones1[0:1, 0:128], start=True, stop=True), sig="te")
                p.w("vector", "te", bt)
                if prev_sr is not None:
                    p.w("vector", "te", prev_sr)
                sv = p.do("vector", lambda e: e.tensor_copy(
                    selone[:], psum[5][0:8, 0:128]), sig="ve")
                p.w("tensor", "ve", sv)
                for i, ti in enumerate(tiles):
                    x1c = x1_of(ti)
                    bank_wait("tensor", 5)
                    sr = prev_sr = p.do(
                        "tensor", lambda e, e_i=e_i, i=i: e.matmul(
                            psum[5][:, 0:T],
                            selone[0:8, 0:128],
                            scorest[0:8, i * T:(i + 1) * T],
                            start=True, stop=True), sig="te")
                    p.w("vector", "te", sr)
                    srv = p.do("vector", lambda e: e.tensor_copy(
                        SSRC[:], psum[5][:, 0:T]), sig="ve")
                    bankw[5] = ("ve", srv)
                    if _os.environ.get("KFFN") == "srep":
                        continue
                    for oc in range(8):
                        par, sem, tok, key = stream(
                            "q1", ("wq1a", "wq1b"),
                            lambda par: WQ1(par).rearrange(
                                "p (k j) -> p k j", k=8),
                            lambda e_i=e_i, oc=oc: dr["w1"][
                                e_i, :, :, oc * 256:(oc + 1) * 256]
                            .rearrange("k p j -> p k j"))
                        p.w("tensor", sem, tok)
                        if (_tap == "wq1" and e_i == 0 and i == 0
                                and oc == 0):
                            p.w("gpsimd", sem, tok)
                            dtp = p.dma("gpsimd", "zob", dr["dbg"][:, 0:2048],
                                        WQ1(par))
                            p.w("tensor", "zob", dtp)
                        lastmm = None
                        for jj in range(2):
                            j = oc * 2 + jj
                            bank = j % 4
                            bank_wait("tensor", bank)
                            for k in range(8):
                                lastmm = mm(
                                    WQ1(par)[:, (k * 2 + jj) * 128:
                                             (k * 2 + jj + 1) * 128],
                                    xn_ch(i, k), bank, k == 0, k == 7,
                                    sig="te" if k == 7 else None)
                            p.w("scalar", "te", lastmm)
                            st = p.do("scalar",
                                      lambda e, j=j, b=bank, ei=e_i:
                                      e.activation(
                                          HT(j), psum[b][:, 0:T], AF.Relu,
                                          bias=b1t[:, ei * 16 + j:
                                                   ei * 16 + j + 1]),
                                      sig="se")
                            bankw[bank] = ("se", st)
                        wpend[key] = ("te", lastmm)
                    for b in range(4):
                        bank_wait("tensor", b)
                    if _os.environ.get("KFFN") == "h":
                        continue
                    if _tap == "h00" and e_i == 0 and i == 0:
                        p.w("gpsimd", "se", p.cnt["se"])
                        dtp = p.dma("gpsimd", "zob", dr["dbg"][:, :],
                                    AR[:, 8240:8240 + 16 * T])
                        p.w("tensor", "zob", dtp)
                    p.w("tensor", "ve", srv)
                    lastmm = None
                    for k in range(16):
                        par, sem, tok, key = stream(
                            "q2", ("wq2a", "wq2b"), WQ2,
                            lambda e_i=e_i, k=k: dr["w2"][e_i, k, :, :])
                        p.w("tensor", sem, tok)
                        for m in range(8):
                            lastmm = mm(WQ2(par)[:, m * 128:(m + 1) * 128],
                                        HT(k), m, k == 0, k == 15,
                                        sig="te" if m == 7 else None)
                        wpend[key] = ("te", lastmm)
                    p.w("vector", "te", lastmm)
                    lastv = None
                    for m in range(8):
                        p.do("vector", lambda e, m=m: e.tensor_tensor(
                            TMP[:], psum[m][:, 0:T], SSRC[:], ALU.mult))
                        lastv = p.do("vector",
                                     lambda e, m=m, x1c=x1c:
                                     e.tensor_tensor(x1c(m), x1c(m), TMP[:],
                                                     ALU.add), sig="ve")
                    for m in range(8):
                        bankw[m] = ("ve", lastv)
                    for m in range(8):
                        bank_wait("tensor", m)
            skip_b2 = routers_only or _os.environ.get("KB2") == "0"
            for i, ti in enumerate(tiles) if not skip_b2 else []:
                x1c = x1_of(ti)
                lt = None
                for m in range(8):
                    bank_wait("tensor", m)
                    lt = p.do("tensor", lambda e, m=m, i=i: e.matmul(
                        psum[m][:, 0:T],
                        b2mmt[0:8, m * 128:(m + 1) * 128],
                        scorest[0:8, i * T:(i + 1) * T],
                        start=True, stop=True),
                        sig="te" if m == 7 else None)
                p.w("vector", "te", lt)
                lastv = None
                for m in range(8):
                    lastv = p.do("vector", lambda e, m=m, x1c=x1c:
                                 e.tensor_tensor(x1c(m), x1c(m),
                                                 psum[m][:, 0:T], ALU.add),
                                 sig="ve")
                for m in range(8):
                    bankw[m] = ("ve", lastv)

        _mph = int(_os.environ.get("KMAMPH", "99"))

        def emit_mamba(ti, is_real):
            rt = ti - wtiles
            if is_real:
                def x1c(c, rt=rt):
                    return x1res[:, c * real + rt * T:
                                 c * real + rt * T + T]
                mu1 = stats[0:1, 0:T]
                r1 = stats[0:1, T:2 * T]
            else:
                p.w("vector", "se", p.cnt["se"])
                lastv = None
                for c in range(DCH):
                    lastv = p.do("vector", lambda e, c=c: e.tensor_copy(
                        X1WM[:, c * T:(c + 1) * T],
                        X1WF[:, c * T:(c + 1) * T]), sig="ve")
                p.w("tensor", "ve", lastv)
                p.w("scalar", "ve", lastv)

                def x1c(c):
                    return X1WM[:, c * T:(c + 1) * T]
                mu1, r1 = SROW(0), SROW(2)
            vtk, stk = ln_stats(x1c, DCH, mu1, r1, SROW(1))
            p.w("vector", "se", stk)
            p.do("vector", lambda e: e.tensor_tensor(
                SROW(3), r1, r1, ALU.mult))
            p.do("vector", lambda e: e.drain())
            p.do("vector", lambda e: e.tensor_tensor(
                SROW(3), SROW(3), SROW(1), ALU.mult))
            p.do("vector", lambda e: e.drain())
            vt = p.do("vector", lambda e: e.tensor_scalar_add(
                SROW(3), SROW(3), EPS), sig="ve")
            p.w("scalar", "ve", vt)
            p.do("scalar", lambda e: e.activation(SROW(3), SROW(3), AF.Ln))
            st = p.do("scalar", lambda e: e.activation(
                SROW(3), SROW(3), AF.Exp, scale=-0.5), sig="se")
            p.w("vector", "se", st)
            vt = p.do("vector", lambda e: e.tensor_tensor(
                SROW(3), SROW(3), r1, ALU.mult), sig="ve")
            p.w("tensor", "ve", vtk)
            p.w("tensor", "ve", vt)
            bcast(mu1, 6)
            bs3 = bcast(SROW(3), 7, sig="te")
            p.w("vector", "te", bs3)
            lastv = None
            for c in range(DCH):
                p.do("vector", lambda e, c=c, x1c=x1c: e.tensor_tensor(
                    XN3(c), x1c(c), psum[6][:, 0:T], ALU.subtract))
                p.do("vector", lambda e, c=c: e.tensor_tensor(
                    XN3(c), XN3(c), psum[7][:, 0:T], ALU.mult))
                lastv = p.do("vector", lambda e, c=c: e.tensor_scalar(
                    XN3(c), XN3(c), lngt[:, c:c + 1], lnbt[:, c:c + 1],
                    ALU.mult, ALU.add), sig="ve")
            p.w("tensor", "ve", lastv)
            lastc = None
            for c in range(ICH):
                lastc = p.do("vector", lambda e, c=c: e.tensor_copy(
                    AR[:, c * 515:c * 515 + 3],
                    carry[:, c * 3:(c + 1) * 3]), sig="ve")
            p.w("tensor", "ve", lastc)
            if not is_real and _mph < 2:
                return
            fence_sync()
            nm = 32 if is_real else 16
            for m_i in range(nm):
                par, sem, tok, key = stream(
                    "ip", ("wipa", "wipb"),
                    lambda par: WSA(par)[:].rearrange(
                        "p (k q) -> p k q", k=8),
                    lambda m_i=m_i: dr["wip"][m_i].rearrange(
                        "k p q -> p k q"))
                p.w("tensor", sem, tok)
                bank = m_i % 2
                bank_wait("tensor", bank)
                lt = None
                for k in range(8):
                    lt = mm(WSA(par)[:, k * 128:(k + 1) * 128], XN3(k),
                            bank, k == 0, k == 7,
                            sig="te" if k == 7 else None)
                wpend[key] = ("te", lt)
                if m_i < 16:
                    p.w("vector", "te", lt)
                    ev = p.do("vector", lambda e, m_i=m_i, b=bank:
                              e.tensor_copy(
                                  AR[:, m_i * 515 + 3:m_i * 515 + 515],
                                  psum[b][:, 0:T]), sig="ve")
                    bankw[bank] = ("ve", ev)
                else:
                    p.w("vector", "te", lt)
                    zsem = "zoa" if bank == 0 else "zob"
                    zkey = ("zo", bank)
                    if zkey in wpend:
                        csem, ctok = wpend[zkey]
                        p.w("vector", csem, ctok)
                    ev = p.do("vector", lambda e, b=bank: e.tensor_copy(
                        ZST(b)[:], psum[b][:, 0:T]), sig="ve")
                    bankw[bank] = ("ve", ev)
                    p.w("gpsimd", "ve", ev)
                    ztok = p.do("gpsimd",
                                lambda e, m_i=m_i, b=bank, rt=rt:
                                e.dma_start(out=dr["zscr"][rt, m_i - 16, :, :],
                                            in_=ZST(b)[:]),
                                sig=zsem, inc=16)
                    wpend[zkey] = (zsem, ztok)
            bank_wait("vector", 0)
            bank_wait("vector", 1)
            if not is_real and _mph < 3:
                return
            lasts = None
            prev_sil = None
            for c in range(ICH):
                if prev_sil is not None:
                    p.w("vector", "se", prev_sil)
                p.do("vector", lambda e, c=c: e.tensor_scalar_mul(
                    TMP[:], XPC(c)[:, 0:512], cwt[:, c * 4:c * 4 + 1]))
                for k in range(1, 4):
                    p.do("vector", lambda e, c=c, k=k:
                         e.scalar_tensor_tensor(
                             TMP[:], XPC(c)[:, k:k + 512],
                             cwt[:, c * 4 + k:c * 4 + k + 1], TMP[:],
                             ALU.mult, ALU.add))
                vt = p.do("vector", lambda e: e.engine_nop(), sig="ve")
                p.w("scalar", "ve", vt)
                lasts = prev_sil = p.do("scalar", lambda e, c=c: e.activation(
                    XC(c), TMP[:], AF.Silu, bias=cbt[:, c:c + 1]), sig="se")
            for c in range(ICH):
                p.do("vector", lambda e, c=c: e.tensor_copy(
                    carry[:, c * 3:(c + 1) * 3],
                    AR[:, c * 515 + 512:c * 515 + 515]))
                p.do("vector", lambda e: e.drain())
            p.w("tensor", "se", lasts)
            bank_wait("tensor", 2)
            lt = None
            for c in range(ICH):
                lt = p.do("tensor", lambda e, c=c: e.matmul(
                    psum[2][0:96, 0:T], xprojw[:, c * 96:(c + 1) * 96],
                    XC(c), start=(c == 0), stop=(c == ICH - 1)),
                    sig="te" if c == ICH - 1 else None)
            p.w("vector", "te", lt)
            vt = p.do("vector", lambda e: e.tensor_copy(
                xdbl[0:96, :], psum[2][0:96, 0:T]), sig="ve")
            p.w("tensor", "ve", vt)
            p.w("scalar", "ve", vt)
            p.w("gpsimd", "ve", vt)
            bankw[2] = None
            if not is_real and _mph < 4:
                return
            lastd = None
            for m_i in range(ICH):
                bank = m_i % 2
                bank_wait("tensor", bank)
                lt = p.do("tensor", lambda e, m_i=m_i, b=bank: e.matmul(
                    psum[b][:, 0:T],
                    dtprojw[:, m_i * 128:(m_i + 1) * 128],
                    xdbl[0:64, :], start=True, stop=True), sig="te")
                p.w("scalar", "te", lt)
                p.do("scalar", lambda e, m_i=m_i, b=bank: e.activation(
                    TMP2[:], psum[b][:, 0:T], AF.Exp,
                    bias=dtbt[:, m_i:m_i + 1]))
                lastd = p.do("scalar", lambda e, m_i=m_i: e.activation(
                    DT_(m_i), TMP2[:], AF.Ln, bias=1.0), sig="se")
                bankw[bank] = ("se", lastd)
            p.w("vector", "se", lastd)
            p.w("tensor", "se", lastd)
            bankw[0] = bankw[1] = None
            if not is_real and _mph < 5:
                return
            ROWB = SC(0)[0:1, 0:T]
            ROWC = SC(1)[0:1, 0:T]
            prev_read = None
            for n in range(NS):
                bank_wait("tensor", 2)
                xb = p.do("tensor", lambda e, n=n: e.matmul(
                    psum[2][0:1, 0:T], selbt[0:96, n:n + 1], xdbl[0:96, :],
                    start=True, stop=True), sig="te")
                if is_real:
                    bank_wait("tensor", 3)
                    xb = p.do("tensor", lambda e, n=n: e.matmul(
                        psum[3][0:1, 0:T], selct[0:96, n:n + 1],
                        xdbl[0:96, :], start=True, stop=True), sig="te")
                p.w("vector", "te", xb)
                rv = p.do("vector", lambda e: e.tensor_copy(
                    ROWB, psum[2][0:1, 0:T]), sig="ve")
                bankw[2] = ("ve", rv)
                if is_real:
                    rv = p.do("vector", lambda e: e.tensor_copy(
                        ROWC, psum[3][0:1, 0:T]), sig="ve")
                    bankw[3] = ("ve", rv)
                if prev_read is not None:
                    p.w("tensor", "ve", prev_read)
                p.w("tensor", "ve", rv)
                bb = bcast(ROWB, 0, sig="te")
                if is_real:
                    bb = bcast(ROWC, 1, sig="te")
                p.w("vector", "te", bb)
                prev_st = None
                for c in range(ICH):
                    p.do("vector", lambda e, c=c: e.tensor_tensor(
                        SC(1), DT_(c), XC(c), ALU.mult))
                    p.do("vector", lambda e: e.tensor_tensor(
                        SC(1), SC(1), psum[0][:, 0:T], ALU.mult))
                    p.do("vector", lambda e, c=c, n=n: e.tensor_scalar_mul(
                        SC(0), DT_(c),
                        atile[:, c * 16 + n:c * 16 + n + 1]))
                    vt = p.do("vector", lambda e: e.engine_nop(), sig="ve")
                    p.w("scalar", "ve", vt)
                    st = p.do("scalar", lambda e: e.activation(
                        SC(2), SC(0), AF.Exp), sig="se")
                    p.w("vector", "se", st)
                    p.do("vector", lambda e, c=c, n=n: e.tensor_tensor_scan(
                        SC(0), SC(2), SC(1),
                        hstate[:, c * 16 + n:c * 16 + n + 1],
                        ALU.mult, ALU.add))
                    p.do("vector", lambda e: e.drain())
                    last = p.do("vector", lambda e, c=c, n=n: e.tensor_copy(
                        hstate[:, c * 16 + n:c * 16 + n + 1],
                        SC(0)[:, 511:512]),
                        sig="ve" if (not is_real and c == ICH - 1) else None)
                    if is_real:
                        if n == 0:
                            last = p.do("vector", lambda e, c=c:
                                        e.tensor_tensor(
                                            YF(c), SC(0), psum[1][:, 0:T],
                                            ALU.mult),
                                        sig="ve" if c == ICH - 1 else None)
                        else:
                            p.do("vector", lambda e: e.tensor_tensor(
                                SC(2), SC(0), psum[1][:, 0:T], ALU.mult))
                            last = p.do("vector", lambda e, c=c:
                                        e.tensor_tensor(
                                            YF(c), YF(c), SC(2), ALU.add),
                                        sig="ve" if c == ICH - 1 else None)
                prev_read = last
            if not is_real:
                return
            fence_sync(gpsimd=True)
            prev_y2mm = None
            prev_sil = None
            zkey = ("zi", 0)
            for c in range(ICH):
                if zkey in wpend:
                    csem, ctok = wpend[zkey]
                    p.w("gpsimd", csem, ctok)
                ztok = p.do("gpsimd", lambda e, c=c, rt=rt:
                            e.dma_start(out=ZINo[:],
                                        in_=dr["zscr"][rt, c, :, :]),
                            sig="zia", inc=16)
                p.w("scalar", "zia", ztok)
                if prev_sil is not None:
                    p.w("scalar", "ve", prev_sil)
                st = p.do("scalar", lambda e: e.activation(
                    ZSo[:], ZINo[:], AF.Silu), sig="se")
                wpend[zkey] = ("se", st)
                if prev_y2mm is not None:
                    p.w("vector", "te", prev_y2mm)
                p.do("vector", lambda e, c=c: e.scalar_tensor_tensor(
                    Y2A[:], XC(c), dvt[:, c:c + 1], YF(c),
                    ALU.mult, ALU.add))
                p.w("vector", "se", st)
                yv = prev_sil = p.do("vector", lambda e: e.tensor_tensor(
                    Y2B[:], Y2A[:], ZSo[:], ALU.mult), sig="ve")
                par, sem, tok, key = stream(
                    "op", ("wopa", "wopb"), WSO,
                    lambda c=c: dr["wop"][c, :, :])
                p.w("tensor", sem, tok)
                p.w("tensor", "ve", yv)
                lastev = None
                for m in range(8):
                    bank = 2 + (m % 6)
                    bank_wait("tensor", bank)
                    prev_y2mm = p.do(
                        "tensor", lambda e, m=m, b=bank, par=par: e.matmul(
                            psum[b][:, 0:T],
                            WSO(par)[:, m * 128:(m + 1) * 128],
                            Y2B[:], start=True, stop=True), sig="te")
                    p.w("vector", "te", prev_y2mm)
                    if c == 0:
                        lastev = p.do("vector", lambda e, m=m, b=bank:
                                      e.tensor_copy(MACC(m),
                                                    psum[b][:, 0:T]),
                                      sig="ve")
                    else:
                        lastev = p.do("vector", lambda e, m=m, b=bank:
                                      e.tensor_tensor(MACC(m), MACC(m),
                                                      psum[b][:, 0:T],
                                                      ALU.add), sig="ve")
                    bankw[bank] = ("ve", lastev)
                wpend[key] = ("te", prev_y2mm)
            p.w("scalar", "ve", lastev)
            lasts = None
            for m in range(8):
                lasts = p.do("scalar", lambda e, m=m: e.activation(
                    MT(m), MACC(m), AF.Gelu), sig="se")
            p.w("tensor", "se", lasts)
            sig_tok = None
            for half in range(2):
                gl = None
                for k in range(8):
                    par, sem, tok, key = stream(
                        "op", ("wopa", "wopb"), WSO,
                        lambda k=k, half=half: dr["wglu"][
                            k, :, half * 1024:(half + 1) * 1024])
                    p.w("tensor", sem, tok)
                    if k == 0:
                        for b in range(8):
                            bank_wait("tensor", b)
                    for ob in range(8):
                        gl = mm(WSO(par)[:, ob * 128:(ob + 1) * 128],
                                MT(k), ob, k == 0, k == 7,
                                sig="te" if ob == 7 else None)
                    wpend[key] = ("te", gl)
                if half == 0:
                    p.w("vector", "te", gl)
                    lastv = None
                    for ob in range(8):
                        lastv = p.do("vector", lambda e, ob=ob:
                                     e.tensor_scalar_add(
                                         G1(ob), psum[ob][:, 0:T],
                                         glubt[:, ob:ob + 1]), sig="ve")
                    for ob in range(8):
                        bankw[ob] = ("ve", lastv)
                else:
                    p.w("scalar", "te", gl)
                    lasts = None
                    for ob in range(8):
                        lasts = p.do("scalar", lambda e, ob=ob:
                                     e.activation(SG(ob), psum[ob][:, 0:T],
                                                  AF.Sigmoid,
                                                  bias=glubt[:, 8 + ob:
                                                             9 + ob]),
                                     sig="se")
                    sig_tok = lasts
                    for ob in range(8):
                        bankw[ob] = ("se", lasts)
            p.w("vector", "se", sig_tok)
            bank_wait("tensor", 6)
            bank_wait("tensor", 7)
            bcast(mu1, 6)
            br1 = bcast(r1, 7, sig="te")
            p.w("vector", "te", br1)
            lastv = None
            for c in range(DCH):
                p.do("vector", lambda e, c=c: e.tensor_tensor(
                    TMP[:], G1(c), SG(c), ALU.mult))
                p.do("vector", lambda e, c=c, x1c=x1c: e.tensor_tensor(
                    XN2S[:], x1c(c), psum[6][:, 0:T], ALU.subtract))
                p.do("vector", lambda e: e.tensor_tensor(
                    XN2S[:], XN2S[:], psum[7][:, 0:T], ALU.mult))
                p.do("vector", lambda e: e.tensor_tensor(
                    TMP[:], TMP[:], XN2S[:], ALU.add))
                lastv = p.do("vector", lambda e, c=c, x1c=x1c:
                             e.tensor_tensor(x1c(c), x1c(c), TMP[:],
                                             ALU.add), sig="ve")
            bankw[6] = ("ve", lastv)
            bankw[7] = ("ve", lastv)

        p.w("vector", "sx", consts_tok)
        p.w("tensor", "sx", consts_tok)
        p.w("tensor", "se", se_a)
        p.w("vector", "se", se_a)
        p.w("gpsimd", "sx", consts_tok)

        def load_x1(ti, is_real):
            rt = ti - wtiles
            if is_real:
                dst = x1res[:].rearrange("p (c s) -> p c s", c=DCH)[
                    :, :, rt * T:(rt + 1) * T]
            else:
                dst = X1WF[:].rearrange("p (c s) -> p c s", c=DCH)
            src = dr["xs"].rearrange("(c p) s -> p c s", p=128)[
                :, :, ti * T:(ti + 1) * T]
            return p.dma("sync", "sx", dst, src)

        def x1w_of(ti):
            return lambda c: X1WF[:, c * T:(c + 1) * T]

        def x1r_of(ti, rt=None):
            rt = ti - wtiles
            return lambda c, rt=rt: x1res[:, c * real + rt * T:
                                          c * real + rt * T + T]

        do_mamba = stage in ("mamba", "full")
        do_mamba_w = stage in ("mamba0", "mamba", "full")
        for ti in range(wtiles):
            xt = load_x1(ti, False)
            for eng in ("vector", "tensor", "scalar"):
                p.w(eng, "sx", xt)
            emit_ffn([ti], x1w_of)
            if do_mamba_w:
                emit_mamba(ti, is_real=False)
        for reg in range(rtiles // 2):
            t0, t1 = wtiles + 2 * reg, wtiles + 2 * reg + 1
            xt = None
            for ti in (t0, t1):
                xt = load_x1(ti, True)
            for eng in ("vector", "tensor", "scalar"):
                p.w(eng, "sx", xt)
            emit_ffn([t0, t1], x1r_of)
            if do_mamba:
                emit_mamba(t0, is_real=True)
                emit_mamba(t1, is_real=True)
        if stage == "full":
            for reg in range(rtiles // 2):
                t0, t1 = wtiles + 2 * reg, wtiles + 2 * reg + 1
                emit_ffn([t0, t1], x1r_of)
        vfin = p.do("vector", lambda e: e.engine_nop(), sig="ve")
        p.w("gpsimd", "ve", vfin)
        otok = p.do("gpsimd", lambda e: e.dma_start(
            out=dr["yout"].rearrange("(c p) s -> p c s", p=128),
            in_=x1res[:].rearrange("p (c s) -> p c s", c=DCH)),
            sig="out", inc=16)
        p.w("gpsimd", "out", otok)

        sems = {}
        for name in p.cnt:
            sems[name] = es.enter_context(nc.semaphore(name))
        block = es.enter_context(nc.Block())
        for engname in B.ENGS:
            def run(eng, q=p.q[engname]):
                for f in q:
                    f(eng, sems)
            getattr(block, engname)(run)

    return nc, p.cnt


_WTILES, _RTILES = 5, 4


def kernel(**inputs):
    w = _stage_weights(inputs)
    spans = _stage_x(inputs["L0_attn_x"], _WTILES, _RTILES)
    nc, _ = build_program(_WTILES, _RTILES)
    in_maps = []
    for core in range(8):
        m = dict(w)
        m["xs"] = spans[core]
        in_maps.append(m)
    res = run_bass_kernel_spmd(nc, in_maps, core_ids=list(range(8)))
    out = np.empty((2, 8192, 1024), np.float32)
    for core in range(8):
        seq, s0 = core // 4, (core % 4) * 2048
        out[seq, s0:s0 + 2048, :] = res.results[core]["yout"].T
    return out


# version 02
# speedup vs baseline: 1.0191x; 1.0191x over previous
import numpy as np
from contextlib import ExitStack

import concourse.bass as bass
import concourse.mybir as mybir
from concourse.bass_utils import run_bass_kernel_spmd

F32 = mybir.dt.float32
AF = mybir.ActivationFunctionType
ALU = mybir.AluOpType
AX = mybir.AxisListType.X

T = 512
DM, DCH = 1024, 8
DI, ICH = 2048, 16
NS = 16
NE, HD, HCH = 8, 2048, 16
EPS = 1e-5



class B:

    ENGS = ("tensor", "vector", "scalar", "gpsimd", "sync")

    def __init__(self):
        self.q = {e: [] for e in self.ENGS}
        self.cnt = {}

    def sem(self, *names):
        for n in names:
            self.cnt[n] = 0

    def do(self, eng, fn, sig=None, inc=1):
        if sig is None:
            self.q[eng].append(lambda e, S, fn=fn: fn(e))
            return None
        self.cnt[sig] += inc
        tok = self.cnt[sig]

        if eng in ("vector", "scalar") and inc == 1:
            def g(e, S, fn=fn, sig=sig, inc=inc):
                fn(e)
                e.drain()
                e.nop().then_inc(S[sig], inc)
        else:
            def g(e, S, fn=fn, sig=sig, inc=inc):
                fn(e).then_inc(S[sig], inc)

        self.q[eng].append(g)
        return tok

    def w(self, eng, sem, tok):
        if tok:
            self.q[eng].append(
                lambda e, S, sem=sem, tok=tok: e.wait_ge(S[sem], tok))

    def dma(self, eng, sem, out, in_):
        return self.do(eng, lambda e, o=out, i=in_: e.dma_start(out=o, in_=i),
                       sig=sem, inc=16)




def _stage_weights(ins):
    f = np.float32

    def c(a):
        return np.ascontiguousarray(a)

    return dict(
        wip=c(np.asarray(ins["L0_attn_in_proj_w"], f)
              .reshape(32, 128, 8, 128).transpose(0, 2, 3, 1)),
        wxp=c(np.asarray(ins["L0_attn_x_proj_w"], f)
              .reshape(96, 16, 128).transpose(1, 2, 0)),
        wdt=c(np.asarray(ins["L0_attn_dt_proj_w"], f)
              .reshape(16, 128, 64).transpose(0, 2, 1)),
        wop=c(np.asarray(ins["L0_attn_out_proj_w"], f)
              .reshape(1024, 16, 128).transpose(1, 2, 0)),
        wglu=c(np.asarray(ins["L0_attn_glu_w"], f)
               .reshape(2048, 8, 128).transpose(1, 2, 0)),
        wg=c(np.asarray(ins["L0_ffn_Wg"], f).reshape(8, 128, 8)),
        w1=c(np.asarray(ins["L0_ffn_W1"], f).reshape(8, 8, 128, 2048)),
        w2=c(np.asarray(ins["L0_ffn_W2"], f).reshape(8, 16, 128, 1024)),
        cw=c(np.asarray(ins["L0_attn_conv_w"], f)
             .reshape(16, 128, 4).transpose(1, 0, 2).reshape(128, 64)),
        cb=c(np.asarray(ins["L0_attn_conv_b"], f).reshape(16, 128).T),
        alog=c(np.asarray(ins["L0_attn_A_log"], f)
               .reshape(16, 128, 16).transpose(1, 0, 2).reshape(128, 256)),
        dvec=c(np.asarray(ins["L0_attn_D"], f).reshape(16, 128).T),
        dtb=c(np.asarray(ins["L0_attn_dt_proj_b"], f).reshape(16, 128).T),
        b1=c(np.asarray(ins["L0_ffn_b1"], f)
             .reshape(8, 16, 128).transpose(2, 0, 1).reshape(128, 128)),
        b2mm=c(np.asarray(ins["L0_ffn_b2"], f)),
        lng=c(np.asarray(ins["L0_attn_ln_g"], f).reshape(8, 128).T),
        lnb=c(np.asarray(ins["L0_attn_ln_b"], f).reshape(8, 128).T),
        glub=c(np.asarray(ins["L0_attn_glu_b"], f).reshape(16, 128).T),
        ident=np.eye(128, dtype=f),
        selb=c(np.eye(96, dtype=f)[:, 64:80]),
        selc=c(np.eye(96, dtype=f)[:, 80:96]),
        d8=c(np.eye(8, dtype=f).reshape(1, 64)),
    )


def _stage_x(x, wtiles, rtiles):
    f = np.float32
    x = np.asarray(x, f)
    W, R = wtiles * T, rtiles * T
    spans = []
    for core in range(8):
        seq, s0 = core // 4, (core % 4) * 2048
        w0 = max(0, s0 - W)
        win = np.concatenate(
            [np.zeros((W - (s0 - w0), DM), f), x[seq, w0:s0, :]], axis=0)
        span = np.concatenate([win, x[seq, s0:s0 + R, :]], axis=0)
        spans.append(np.ascontiguousarray(span.T))
    return spans




def build_program(wtiles=5, rtiles=4, stage="full"):
    ntiles = wtiles + rtiles
    span = ntiles * T
    real = rtiles * T

    nc = bass.Bass("TRN2", target_bir_lowering=False,
                   detect_race_conditions=False)
    dr = {}
    for name, shape in [
            ("xs", [DM, span]), ("wip", [32, 8, 128, 128]),
            ("wxp", [16, 128, 96]), ("wdt", [16, 64, 128]),
            ("wop", [16, 128, 1024]), ("wglu", [8, 128, 2048]),
            ("wg", [8, 128, 8]), ("w1", [8, 8, 128, 2048]),
            ("w2", [8, 16, 128, 1024]), ("cw", [128, 64]), ("cb", [128, 16]),
            ("alog", [128, 256]), ("dvec", [128, 16]), ("dtb", [128, 16]),
            ("b1", [128, 128]), ("b2mm", [8, 1024]),
            ("lng", [128, 8]), ("lnb", [128, 8]), ("glub", [128, 16]),
            ("ident", [128, 128]), ("d8", [1, 64]),
            ("selb", [96, 16]), ("selc", [96, 16])]:
        dr[name] = nc.dram_tensor(name, shape, F32, kind="ExternalInput").ap()
    dr["zscr"] = nc.dram_tensor("zscr", [rtiles, ICH, 128, T], F32,
                                kind="Internal").ap()
    dr["yout"] = nc.dram_tensor("yout", [DM, real], F32,
                                kind="ExternalOutput").ap()
    import os as _os
    _tap = _os.environ.get("KTAP", "")
    if _tap:
        dr["dbg"] = nc.dram_tensor("dbg", [128, 16 * T], F32,
                                   kind="ExternalOutput").ap()

    p = B()
    p.sem("sx", "wq1a", "wq1b", "wq2a", "wq2b", "wipa", "wipb",
          "wopa", "wopb", "zoa", "zob", "zia", "zib", "out",
          "te", "ve", "se", "ge")

    es = ExitStack()
    with es:
        sb = {}

        def alloc(name, shape):
            sb[name] = es.enter_context(nc.sbuf_tensor(name, shape, F32))
            return sb[name]

        x1res = alloc("x1res", [128, DCH * real])
        hstate = alloc("hstate", [128, 256])
        atile = alloc("atile", [128, 256])
        xprojw = alloc("xprojw", [128, 16 * 96])
        dtprojw = alloc("dtprojw", [64, 16 * 128])
        wgw = alloc("wgw", [128, 64])
        identt = alloc("identt", [128, 128])
        stats = alloc("stats", [128, 2 * T])
        scorest = alloc("scorest", [128, 2 * T])
        cwt = alloc("cwt", [128, 64])
        cbt = alloc("cbt", [128, 16])
        dvt = alloc("dvt", [128, 16])
        dtbt = alloc("dtbt", [128, 16])
        b1t = alloc("b1t", [128, 128])
        b2mmt = alloc("b2mmt", [8, 1024])
        lngt = alloc("lngt", [128, 8])
        lnbt = alloc("lnbt", [128, 8])
        glubt = alloc("glubt", [128, 16])
        selone = alloc("selone", [8, 128])
        d8t = alloc("d8t", [1, 64])
        onesc = alloc("onesc", [128, 1])
        ones1 = alloc("ones1", [1, 128])
        xdbl = alloc("xdbl", [128, 512])
        carry = alloc("carry", [128, 48])
        selbt = alloc("selbt", [96, 16])
        selct = alloc("selct", [96, 16])
        AR = alloc("arena", [128, 26192])

        def ar(off, size):
            assert off + size <= 26192
            return AR[:, off:off + size]

        def xn_ch(i, c):
            return AR[:, i * 4096 + c * T: i * 4096 + (c + 1) * T]

        X1WF = ar(4096, 4096)
        HT = lambda j: ar(8240 + j * T, T)
        LOGT = AR[0:8, 8240:8240 + T]
        WQ1 = lambda par: ar(16432 + par * 2048, 2048)
        WQ2 = lambda par: ar(20528 + par * 1024, 1024)
        SSRC = ar(22576, T)
        TMP = ar(23088, T)
        TMP2 = ar(23600, T)
        SROW = lambda i: AR[0:1, 24112 + i * T: 24112 + (i + 1) * T]
        RS = lambda i: AR[:, 26160 + i: 26161 + i]

        XPC = lambda c: AR[:, c * 515:(c + 1) * 515]
        DT_ = lambda c: AR[:, c * 515: c * 515 + 512]
        XC = lambda c: ar(8240 + c * T, T)
        X1WM = ar(8240, 4096)
        XN3 = lambda c: ar(16432 + c * T, T)
        WSA = lambda par: ar(20528 + par * 1024, 1024)
        ZST = lambda par: ar(12336 + par * T, T)
        SC = lambda i: ar(16432 + i * T, T)
        YF = lambda c: ar(17968 + c * T, T)
        WSO = lambda par: ar(par * 1024, 1024)
        Y2A = ar(2048, T)
        Y2B = ar(2560, T)
        ZSo = ar(3072, T)
        ZINo = ar(3584, T)
        MACC = lambda m: ar(4096 + m * T, T)
        MT = lambda m: ar(8240 + m * T, T)
        G1 = lambda ob: ar(12336 + ob * T, T)
        SG = lambda ob: ar(16432 + ob * T, T)
        XN2S = ar(20528, T)

        psum = [es.enter_context(nc.psum_tensor(f"pb{i}", [128, T], F32))
                for i in range(8)]

        bankw = [None] * 8

        def bank_wait(eng, b):
            if bankw[b] is not None:
                sem, tok = bankw[b]
                p.w(eng, sem, tok)
                bankw[b] = None

        def mm(lhsT, rhs, bank, start, stop, sig=None, mrows=128):
            return p.do("tensor",
                        lambda e, o=psum[bank][0:mrows, 0:T], l=lhsT, r=rhs,
                        st=start, sp=stop:
                        e.matmul(o, l, r, start=st, stop=sp), sig=sig)

        def recip_safe(dst, srcap, k1, k2):
            vt = p.do("vector", lambda e: e.engine_nop(), sig="ve")
            p.w("scalar", "ve", vt)
            p.do("scalar", lambda e: e.activation(RS(k1), srcap, AF.Square))
            p.do("scalar", lambda e: e.activation(RS(k2), RS(k1), AF.Ln))
            st = p.do("scalar", lambda e: e.activation(
                RS(k1), RS(k2), AF.Exp, scale=-1.0), sig="se")
            p.w("vector", "se", st)
            p.do("vector", lambda e: e.tensor_tensor(
                dst, srcap, RS(k1), ALU.mult))
            p.do("vector", lambda e: e.drain())

        def bcast(row, bank, sig=None):
            return p.do("tensor", lambda e, r=row, b=bank:
                        e.matmul(psum[b][:, 0:T], ones1[:], r,
                                 start=True, stop=True), sig=sig)

        p.do("vector", lambda e: e.memset(AR[:, 0:13096], 0.0))
        for gi in range(30):
            sl = (slice(13096, 26192) if gi % 2 == 0 else slice(0, 13096))
            s2 = (slice(0, 13096) if gi % 2 == 0 else slice(13096, 26192))
            p.do("vector", lambda e, sl=sl, s2=s2: e.tensor_copy(
                AR[:, sl], AR[:, s2]))
        guard = p.do("vector", lambda e: e.engine_nop(), sig="ve")
        p.w("tensor", "ve", guard)
        p.w("scalar", "ve", guard)
        p.w("gpsimd", "ve", guard)
        p.do("vector", lambda e: e.memset(onesc[:], 1.0))
        p.do("vector", lambda e: e.memset(ones1[:], 1.0))
        p.do("vector", lambda e: e.memset(hstate[:], 0.0))
        p.do("vector", lambda e: e.memset(carry[:], 0.0))
        p.do("vector", lambda e: e.drain())

        for name, dst, resh in [
                ("wxp", xprojw, ("k p j -> p k j", "p (k j) -> p k j", 16)),
                ("wdt", dtprojw, ("m p q -> p m q", "p (m q) -> p m q", 16)),
                ("wg", wgw, ("k p j -> p k j", "p (k j) -> p k j", 8))]:
            s_pat, d_pat, kk = resh
            p.dma("sync", "sx", dst[:].rearrange(d_pat, **(
                {"k": kk} if "k" in d_pat.split("->")[1] else {"m": kk})),
                dr[name].rearrange(s_pat))
        for name, dst in [("ident", identt), ("cw", cwt), ("cb", cbt),
                          ("dvec", dvt), ("dtb", dtbt), ("b1", b1t),
                          ("b2mm", b2mmt), ("lng", lngt), ("lnb", lnbt),
                          ("glub", glubt), ("d8", d8t),
                          ("selb", selbt), ("selc", selct)]:
            p.dma("sync", "sx", dst[:], dr[name][:, :])
        p.w("sync", "ve", p.cnt["ve"])
        alog_tok = p.dma("sync", "sx", atile[:], dr["alog"][:, :])
        p.w("scalar", "sx", alog_tok)
        p.do("scalar", lambda e: e.activation(atile[:], atile[:], AF.Exp))
        se_a = p.do("scalar", lambda e: e.mul(atile[:], atile[:], -1.0),
                    sig="se")
        consts_tok = p.cnt["sx"]

        def ln_stats(xch, nch, mu_row, r_row, var_row):
            bank_wait("tensor", 7)
            p.w("scalar", "ve", p.cnt["ve"])
            prev_mm = None
            for c in range(nch):
                if prev_mm is not None:
                    p.w("scalar", "te", prev_mm)
                sq = p.do("scalar", lambda e, c=c: e.activation(
                    TMP, xch(c), AF.Square), sig="se")
                p.do("tensor", lambda e, c=c, n=nch: e.matmul(
                    psum[7][0:1, 0:T], onesc[:], xch(c),
                    start=(c == 0), stop=(c == n - 1)))
                p.w("tensor", "se", sq)
                prev_mm = p.do("tensor", lambda e, c=c, n=nch: e.matmul(
                    psum[7][32:33, 0:T], onesc[:], TMP[:],
                    start=(c == 0), stop=(c == n - 1)), sig="te")
            p.w("vector", "te", prev_mm)
            nf = float(nch * 128)
            p.do("vector", lambda e: e.tensor_scalar_mul(
                mu_row, psum[7][0:1, 0:T], 1.0 / nf))
            p.do("vector", lambda e: e.tensor_scalar_mul(
                var_row, psum[7][32:33, 0:T], 1.0 / nf))
            p.do("vector", lambda e: e.tensor_tensor(
                TMP[0:1, 0:T], mu_row, mu_row, ALU.mult))
            p.do("vector", lambda e: e.drain())
            p.do("vector", lambda e: e.tensor_tensor(
                var_row, var_row, TMP[0:1, 0:T], ALU.subtract))
            p.do("vector", lambda e: e.drain())
            ve_tok = p.do("vector", lambda e: e.tensor_scalar_add(
                TMP[0:1, 0:T], var_row, EPS), sig="ve")
            p.w("scalar", "ve", ve_tok)
            p.do("scalar", lambda e: e.activation(
                TMP2[0:1, 0:T], TMP[0:1, 0:T], AF.Ln))
            se_tok = p.do("scalar", lambda e: e.activation(
                r_row, TMP2[0:1, 0:T], AF.Exp, scale=-0.5), sig="se")
            return ve_tok, se_tok

        def router(xnch, tile_col):
            bank_wait("tensor", 5)
            bank_wait("tensor", 6)
            lt = None
            for k in range(DCH):
                lt = mm(wgw[:, k * 8:(k + 1) * 8], xnch(k), 5,
                        k == 0, k == DCH - 1,
                        sig="te" if k == DCH - 1 else None, mrows=8)
            p.w("vector", "te", lt)
            vt = p.do("vector", lambda e: e.tensor_copy(
                LOGT, psum[5][0:8, 0:T]), sig="ve")
            PT = TMP[:, 0:8]
            EX = TMP[:, 16:24]
            SG1 = TMP[:, 32:40]
            EQ1 = TMP[:, 48:56]
            PM = TMP[:, 64:72]
            SG2 = TMP[:, 80:88]
            EQ2 = TMP[:, 96:104]
            T4 = TMP[:, 112:116]
            T2 = TMP[:, 128:130]
            SC1 = TMP[:, 144:152]
            SC2 = TMP[:, 160:168]
            SCR = TMP[:, 176:184]

            def pv(fn):
                p.do("vector", fn)
                p.do("vector", lambda e: e.drain())

            def tree8(srcfn, dst, op):
                pv(lambda e, s=srcfn, op=op: e.tensor_tensor(
                    T4, s()[:, 0:4], s()[:, 4:8], op))
                pv(lambda e, op=op: e.tensor_tensor(
                    T2, T4[:, 0:2], T4[:, 2:4], op))
                pv(lambda e, d=dst, op=op: e.tensor_tensor(
                    d, T2[:, 0:1], T2[:, 1:2], op))

            ptv = vt
            for tg in range(4):
                sl = slice(tg * 128, (tg + 1) * 128)
                p.w("tensor", "ve", ptv)
                tt = p.do("tensor", lambda e, sl=sl: e.transpose(
                    psum[6][:, 0:8], LOGT[:, sl], identt[0:8, 0:8]),
                    sig="te")
                p.w("vector", "te", tt)
                ptv = p.do("vector", lambda e: e.tensor_copy(
                    PT, psum[6][:, 0:8]), sig="ve")
                tree8(lambda: PT, RS(0), ALU.max)
                v2 = p.do("vector", lambda e: e.tensor_scalar_mul(
                    RS(1), RS(0), -1.0), sig="ve")
                if _tap == "rs1" and tg == 3 and tile_col == T:
                    p.w("gpsimd", "ve", v2)
                    p.dma("gpsimd", "zob", dr["dbg"][0:128, 0:192],
                          TMP[:, 0:192])
                    zt = p.dma("gpsimd", "zob", dr["dbg"][0:128, 192:224],
                               AR[:, 26160:26160 + 32])
                    p.w("vector", "zob", zt)
                p.w("scalar", "ve", v2)
                st = p.do("scalar", lambda e: e.activation(
                    EX, PT, AF.Exp, bias=RS(1)), sig="se")
                p.w("vector", "se", st)
                tree8(lambda: EX, RS(4), ALU.max)
                v2 = p.do("vector", lambda e: e.tensor_scalar_mul(
                    RS(5), RS(4), -1.0), sig="ve")
                p.w("scalar", "ve", v2)
                st = p.do("scalar", lambda e: e.activation(
                    SG1, EX, AF.Sign, bias=RS(5)), sig="se")
                p.w("vector", "se", st)
                pv(lambda e: e.tensor_scalar_add(EQ1, SG1, 1.0))
                tree8(lambda: EQ1, RS(6), ALU.add)
                pv(lambda e: e.scalar_tensor_tensor(
                    PM, EQ1, -2.0, EX, ALU.mult, ALU.add))
                tree8(lambda: PM, RS(7), ALU.max)
                v2 = p.do("vector", lambda e: e.tensor_scalar_mul(
                    RS(8), RS(7), -1.0), sig="ve")
                p.w("scalar", "ve", v2)
                st = p.do("scalar", lambda e: e.activation(
                    SG2, PM, AF.Sign, bias=RS(8)), sig="se")
                p.w("vector", "se", st)
                pv(lambda e: e.tensor_scalar_add(EQ2, SG2, 1.0))
                tree8(lambda: EQ2, RS(9), ALU.add)
                pv(lambda e: e.tensor_scalar_add(RS(10), RS(6), -1.0))
                pv(lambda e: e.tensor_scalar_min(RS(11), RS(10), 1.0))
                pv(lambda e: e.tensor_tensor(
                    RS(26), RS(4), RS(7), ALU.add))
                pv(lambda e: e.tensor_scalar_add(RS(12), RS(26), 1e-15))
                recip_safe(RS(13), RS(12), 22, 23)
                recip_safe(RS(14), RS(6), 22, 23)
                recip_safe(RS(15), RS(9), 22, 23)
                pv(lambda e: e.tensor_scalar(
                    RS(16), RS(11), -1.0, 1.0, ALU.mult, ALU.add))
                pv(lambda e: e.tensor_tensor(RS(17), RS(4), RS(13), ALU.mult))
                pv(lambda e: e.tensor_tensor(
                    RS(18), RS(17), RS(16), ALU.mult))
                pv(lambda e: e.tensor_tensor(
                    RS(19), RS(11), RS(14), ALU.mult))
                pv(lambda e: e.tensor_tensor(
                    RS(2), RS(18), RS(19), ALU.add))
                pv(lambda e: e.tensor_tensor(RS(20), RS(7), RS(13), ALU.mult))
                pv(lambda e: e.tensor_tensor(
                    RS(21), RS(20), RS(16), ALU.mult))
                pv(lambda e: e.tensor_tensor(
                    RS(3), RS(21), RS(15), ALU.mult))
                pv(lambda e: e.tensor_scalar_mul(RS(24), RS(2), 0.5))
                v2 = p.do("vector", lambda e: e.tensor_scalar_mul(
                    RS(25), RS(3), 0.5), sig="ve")
                pv(lambda e: e.tensor_scalar_mul(SC1, EQ1, RS(24)))
                pv(lambda e: e.tensor_scalar_mul(SC2, EQ2, RS(25)))
                v2 = p.do("vector", lambda e: e.tensor_tensor(
                    SCR, SC1, SC2, ALU.add), sig="ve")
                p.w("tensor", "ve", v2)
                bank_wait("tensor", 7)
                tt = p.do("tensor", lambda e: e.transpose(
                    psum[7][0:8, 0:128], SCR, identt[:]), sig="te")
                p.w("vector", "te", tt)
                scv = p.do("vector", lambda e, tg=tg, tc=tile_col:
                           e.tensor_copy(
                               scorest[0:8, tc + tg * 128:
                                       tc + (tg + 1) * 128],
                               psum[7][0:8, 0:128]), sig="ve")
                bankw[7] = ("ve", scv)
            p.w("scalar", "te", tt)
            p.w("vector", "te", tt)
            if _tap == "scores":
                p.w("gpsimd", "ve", p.cnt["ve"])
                p.dma("gpsimd", "zob", dr["dbg"][0:8, 0:2 * T], scorest[0:8, :])
                p.dma("gpsimd", "zob", dr["dbg"][0:128, 2 * T:2 * T + 192],
                      TMP[:, 0:192])
                p.dma("gpsimd", "zob",
                      dr["dbg"][0:128, 2 * T + 192:2 * T + 224],
                      AR[:, 26160:26160 + 32])
            return p.do("vector", lambda e: e.engine_nop(), sig="ve")

        wpar = {"q1": 0, "q2": 0, "ip": 0, "op": 0}
        wpend = {}

        def stream(tag, sems, buf_of, src_of):
            par = wpar[tag]
            sem = sems[par]
            key = (tag, par)
            if key in wpend:
                csem, ctok = wpend[key]
                p.w("sync", csem, ctok)
            tok = p.dma("sync", sem, buf_of(par), src_of())
            wpar[tag] ^= 1
            return par, sem, tok, key

        def fence_sync(gpsimd=False):
            for s in ("te", "ve", "se"):
                p.w("sync", s, p.cnt[s])
                if gpsimd:
                    p.w("gpsimd", s, p.cnt[s])

        def emit_ffn(tiles, x1_of):
            fence_sync()
            routers_only = _os.environ.get("KFFN") == "ln"
            n_experts = int(_os.environ.get("KNE", str(NE)))
            for i, ti in enumerate(tiles):
                xch = x1_of(ti)
                vtk, stk = ln_stats(xch, DCH, SROW(0), SROW(2), SROW(1))
                p.w("tensor", "ve", vtk)
                p.w("tensor", "se", stk)
                bcast(SROW(0), 6)
                brs = bcast(SROW(2), 7, sig="te")
                p.w("vector", "te", brs)
                lastv = None
                for c in range(DCH):
                    p.do("vector", lambda e, i=i, c=c, xch=xch:
                         e.tensor_tensor(xn_ch(i, c), xch(c),
                                         psum[6][:, 0:T], ALU.subtract))
                    lastv = p.do("vector", lambda e, i=i, c=c:
                                 e.tensor_tensor(xn_ch(i, c), xn_ch(i, c),
                                                 psum[7][:, 0:T], ALU.mult),
                                 sig="ve" if c == DCH - 1 else None)
                p.w("tensor", "ve", lastv)
                router(lambda c, i=i: xn_ch(i, c), i * T)
            prev_sr = None
            for e_i in range(0 if routers_only else n_experts):
                bank_wait("tensor", 5)
                bt = p.do("tensor", lambda e, e_i=e_i: e.matmul(
                    psum[5][0:8, 0:128], d8t[0:1, e_i * 8:e_i * 8 + 8],
                    ones1[0:1, 0:128], start=True, stop=True), sig="te")
                p.w("vector", "te", bt)
                if prev_sr is not None:
                    p.w("vector", "te", prev_sr)
                sv = p.do("vector", lambda e: e.tensor_copy(
                    selone[:], psum[5][0:8, 0:128]), sig="ve")
                p.w("tensor", "ve", sv)
                for i, ti in enumerate(tiles):
                    x1c = x1_of(ti)
                    bank_wait("tensor", 5)
                    sr = prev_sr = p.do(
                        "tensor", lambda e, e_i=e_i, i=i: e.matmul(
                            psum[5][:, 0:T],
                            selone[0:8, 0:128],
                            scorest[0:8, i * T:(i + 1) * T],
                            start=True, stop=True), sig="te")
                    p.w("vector", "te", sr)
                    srv = p.do("vector", lambda e: e.tensor_copy(
                        SSRC[:], psum[5][:, 0:T]), sig="ve")
                    bankw[5] = ("ve", srv)
                    if _os.environ.get("KFFN") == "srep":
                        continue
                    for oc in range(8):
                        par, sem, tok, key = stream(
                            "q1", ("wq1a", "wq1b"),
                            lambda par: WQ1(par).rearrange(
                                "p (k j) -> p k j", k=8),
                            lambda e_i=e_i, oc=oc: dr["w1"][
                                e_i, :, :, oc * 256:(oc + 1) * 256]
                            .rearrange("k p j -> p k j"))
                        p.w("tensor", sem, tok)
                        if (_tap == "wq1" and e_i == 0 and i == 0
                                and oc == 0):
                            p.w("gpsimd", sem, tok)
                            dtp = p.dma("gpsimd", "zob", dr["dbg"][:, 0:2048],
                                        WQ1(par))
                            p.w("tensor", "zob", dtp)
                        lastmm = None
                        for jj in range(2):
                            j = oc * 2 + jj
                            bank = j % 4
                            bank_wait("tensor", bank)
                            for k in range(8):
                                lastmm = mm(
                                    WQ1(par)[:, (k * 2 + jj) * 128:
                                             (k * 2 + jj + 1) * 128],
                                    xn_ch(i, k), bank, k == 0, k == 7,
                                    sig="te" if k == 7 else None)
                            p.w("scalar", "te", lastmm)
                            st = p.do("scalar",
                                      lambda e, j=j, b=bank, ei=e_i:
                                      e.activation(
                                          HT(j), psum[b][:, 0:T], AF.Relu,
                                          bias=b1t[:, ei * 16 + j:
                                                   ei * 16 + j + 1]),
                                      sig="se")
                            bankw[bank] = ("se", st)
                        wpend[key] = ("te", lastmm)
                    for b in range(4):
                        bank_wait("tensor", b)
                    if _os.environ.get("KFFN") == "h":
                        continue
                    if _tap == "h00" and e_i == 0 and i == 0:
                        p.w("gpsimd", "se", p.cnt["se"])
                        dtp = p.dma("gpsimd", "zob", dr["dbg"][:, :],
                                    AR[:, 8240:8240 + 16 * T])
                        p.w("tensor", "zob", dtp)
                    p.w("tensor", "ve", srv)
                    lastmm = None
                    for k in range(16):
                        par, sem, tok, key = stream(
                            "q2", ("wq2a", "wq2b"), WQ2,
                            lambda e_i=e_i, k=k: dr["w2"][e_i, k, :, :])
                        p.w("tensor", sem, tok)
                        for m in range(8):
                            lastmm = mm(WQ2(par)[:, m * 128:(m + 1) * 128],
                                        HT(k), m, k == 0, k == 15,
                                        sig="te" if m == 7 else None)
                        wpend[key] = ("te", lastmm)
                    p.w("vector", "te", lastmm)
                    lastv = None
                    for m in range(8):
                        p.do("vector", lambda e, m=m: e.tensor_tensor(
                            TMP[:], psum[m][:, 0:T], SSRC[:], ALU.mult))
                        lastv = p.do("vector",
                                     lambda e, m=m, x1c=x1c:
                                     e.tensor_tensor(x1c(m), x1c(m), TMP[:],
                                                     ALU.add), sig="ve")
                    for m in range(8):
                        bankw[m] = ("ve", lastv)
                    for m in range(8):
                        bank_wait("tensor", m)
            skip_b2 = routers_only or _os.environ.get("KB2") == "0"
            for i, ti in enumerate(tiles) if not skip_b2 else []:
                x1c = x1_of(ti)
                lt = None
                for m in range(8):
                    bank_wait("tensor", m)
                    lt = p.do("tensor", lambda e, m=m, i=i: e.matmul(
                        psum[m][:, 0:T],
                        b2mmt[0:8, m * 128:(m + 1) * 128],
                        scorest[0:8, i * T:(i + 1) * T],
                        start=True, stop=True),
                        sig="te" if m == 7 else None)
                p.w("vector", "te", lt)
                lastv = None
                for m in range(8):
                    lastv = p.do("vector", lambda e, m=m, x1c=x1c:
                                 e.tensor_tensor(x1c(m), x1c(m),
                                                 psum[m][:, 0:T], ALU.add),
                                 sig="ve")
                for m in range(8):
                    bankw[m] = ("ve", lastv)

        _mph = int(_os.environ.get("KMAMPH", "99"))
        wcopy_tok = [None]

        def emit_mamba(ti, is_real):
            rt = ti - wtiles
            if is_real:
                def x1c(c, rt=rt):
                    return x1res[:, c * real + rt * T:
                                 c * real + rt * T + T]
                mu1 = stats[0:1, 0:T]
                r1 = stats[0:1, T:2 * T]
            else:
                p.w("vector", "se", p.cnt["se"])
                lastv = None
                for c in range(DCH):
                    lastv = p.do("vector", lambda e, c=c: e.tensor_copy(
                        X1WM[:, c * T:(c + 1) * T],
                        X1WF[:, c * T:(c + 1) * T]), sig="ve")
                p.w("tensor", "ve", lastv)
                p.w("scalar", "ve", lastv)
                wcopy_tok[0] = lastv

                def x1c(c):
                    return X1WM[:, c * T:(c + 1) * T]
                mu1, r1 = SROW(0), SROW(2)
            vtk, stk = ln_stats(x1c, DCH, mu1, r1, SROW(1))
            p.w("vector", "se", stk)
            p.do("vector", lambda e: e.tensor_tensor(
                SROW(3), r1, r1, ALU.mult))
            p.do("vector", lambda e: e.drain())
            p.do("vector", lambda e: e.tensor_tensor(
                SROW(3), SROW(3), SROW(1), ALU.mult))
            p.do("vector", lambda e: e.drain())
            vt = p.do("vector", lambda e: e.tensor_scalar_add(
                SROW(3), SROW(3), EPS), sig="ve")
            p.w("scalar", "ve", vt)
            p.do("scalar", lambda e: e.activation(SROW(3), SROW(3), AF.Ln))
            st = p.do("scalar", lambda e: e.activation(
                SROW(3), SROW(3), AF.Exp, scale=-0.5), sig="se")
            p.w("vector", "se", st)
            vt = p.do("vector", lambda e: e.tensor_tensor(
                SROW(3), SROW(3), r1, ALU.mult), sig="ve")
            p.w("tensor", "ve", vtk)
            p.w("tensor", "ve", vt)
            bcast(mu1, 6)
            bs3 = bcast(SROW(3), 7, sig="te")
            p.w("vector", "te", bs3)
            lastv = None
            for c in range(DCH):
                p.do("vector", lambda e, c=c, x1c=x1c: e.tensor_tensor(
                    XN3(c), x1c(c), psum[6][:, 0:T], ALU.subtract))
                p.do("vector", lambda e, c=c: e.tensor_tensor(
                    XN3(c), XN3(c), psum[7][:, 0:T], ALU.mult))
                lastv = p.do("vector", lambda e, c=c: e.tensor_scalar(
                    XN3(c), XN3(c), lngt[:, c:c + 1], lnbt[:, c:c + 1],
                    ALU.mult, ALU.add), sig="ve")
            p.w("tensor", "ve", lastv)
            lastc = None
            for c in range(ICH):
                lastc = p.do("vector", lambda e, c=c: e.tensor_copy(
                    AR[:, c * 515:c * 515 + 3],
                    carry[:, c * 3:(c + 1) * 3]), sig="ve")
            p.w("tensor", "ve", lastc)
            if not is_real and _mph < 2:
                return
            fence_sync()
            nm = 32 if is_real else 16
            for m_i in range(nm):
                par, sem, tok, key = stream(
                    "ip", ("wipa", "wipb"),
                    lambda par: WSA(par)[:].rearrange(
                        "p (k q) -> p k q", k=8),
                    lambda m_i=m_i: dr["wip"][m_i].rearrange(
                        "k p q -> p k q"))
                p.w("tensor", sem, tok)
                bank = m_i % 2
                bank_wait("tensor", bank)
                lt = None
                for k in range(8):
                    lt = mm(WSA(par)[:, k * 128:(k + 1) * 128], XN3(k),
                            bank, k == 0, k == 7,
                            sig="te" if k == 7 else None)
                wpend[key] = ("te", lt)
                if m_i < 16:
                    p.w("vector", "te", lt)
                    ev = p.do("vector", lambda e, m_i=m_i, b=bank:
                              e.tensor_copy(
                                  AR[:, m_i * 515 + 3:m_i * 515 + 515],
                                  psum[b][:, 0:T]), sig="ve")
                    bankw[bank] = ("ve", ev)
                else:
                    p.w("vector", "te", lt)
                    zsem = "zoa" if bank == 0 else "zob"
                    zkey = ("zo", bank)
                    if zkey in wpend:
                        csem, ctok = wpend[zkey]
                        p.w("vector", csem, ctok)
                    ev = p.do("vector", lambda e, b=bank: e.tensor_copy(
                        ZST(b)[:], psum[b][:, 0:T]), sig="ve")
                    bankw[bank] = ("ve", ev)
                    p.w("gpsimd", "ve", ev)
                    ztok = p.do("gpsimd",
                                lambda e, m_i=m_i, b=bank, rt=rt:
                                e.dma_start(out=dr["zscr"][rt, m_i - 16, :, :],
                                            in_=ZST(b)[:]),
                                sig=zsem, inc=16)
                    wpend[zkey] = (zsem, ztok)
            bank_wait("vector", 0)
            bank_wait("vector", 1)
            if not is_real and _mph < 3:
                return
            lasts = None
            prev_sil = None
            for c in range(ICH):
                if prev_sil is not None:
                    p.w("vector", "se", prev_sil)
                p.do("vector", lambda e, c=c: e.tensor_scalar_mul(
                    TMP[:], XPC(c)[:, 0:512], cwt[:, c * 4:c * 4 + 1]))
                for k in range(1, 4):
                    p.do("vector", lambda e, c=c, k=k:
                         e.scalar_tensor_tensor(
                             TMP[:], XPC(c)[:, k:k + 512],
                             cwt[:, c * 4 + k:c * 4 + k + 1], TMP[:],
                             ALU.mult, ALU.add))
                vt = p.do("vector", lambda e: e.engine_nop(), sig="ve")
                p.w("scalar", "ve", vt)
                lasts = prev_sil = p.do("scalar", lambda e, c=c: e.activation(
                    XC(c), TMP[:], AF.Silu, bias=cbt[:, c:c + 1]), sig="se")
            for c in range(ICH):
                p.do("vector", lambda e, c=c: e.tensor_copy(
                    carry[:, c * 3:(c + 1) * 3],
                    AR[:, c * 515 + 512:c * 515 + 515]))
                p.do("vector", lambda e: e.drain())
            p.w("tensor", "se", lasts)
            bank_wait("tensor", 2)
            lt = None
            for c in range(ICH):
                lt = p.do("tensor", lambda e, c=c: e.matmul(
                    psum[2][0:96, 0:T], xprojw[:, c * 96:(c + 1) * 96],
                    XC(c), start=(c == 0), stop=(c == ICH - 1)),
                    sig="te" if c == ICH - 1 else None)
            p.w("vector", "te", lt)
            vt = p.do("vector", lambda e: e.tensor_copy(
                xdbl[0:96, :], psum[2][0:96, 0:T]), sig="ve")
            p.w("tensor", "ve", vt)
            p.w("scalar", "ve", vt)
            p.w("gpsimd", "ve", vt)
            bankw[2] = None
            if not is_real and _mph < 4:
                return
            lastd = None
            for m_i in range(ICH):
                bank = m_i % 2
                bank_wait("tensor", bank)
                lt = p.do("tensor", lambda e, m_i=m_i, b=bank: e.matmul(
                    psum[b][:, 0:T],
                    dtprojw[:, m_i * 128:(m_i + 1) * 128],
                    xdbl[0:64, :], start=True, stop=True), sig="te")
                p.w("scalar", "te", lt)
                p.do("scalar", lambda e, m_i=m_i, b=bank: e.activation(
                    TMP2[:], psum[b][:, 0:T], AF.Exp,
                    bias=dtbt[:, m_i:m_i + 1]))
                lastd = p.do("scalar", lambda e, m_i=m_i: e.activation(
                    DT_(m_i), TMP2[:], AF.Ln, bias=1.0), sig="se")
                bankw[bank] = ("se", lastd)
            p.w("vector", "se", lastd)
            p.w("tensor", "se", lastd)
            bankw[0] = bankw[1] = None
            if not is_real and _mph < 5:
                return
            ROWB = SC(0)[0:1, 0:T]
            ROWC = SC(1)[0:1, 0:T]
            prev_read = None
            for n in range(NS):
                bank_wait("tensor", 2)
                xb = p.do("tensor", lambda e, n=n: e.matmul(
                    psum[2][0:1, 0:T], selbt[0:96, n:n + 1], xdbl[0:96, :],
                    start=True, stop=True), sig="te")
                if is_real:
                    bank_wait("tensor", 3)
                    xb = p.do("tensor", lambda e, n=n: e.matmul(
                        psum[3][0:1, 0:T], selct[0:96, n:n + 1],
                        xdbl[0:96, :], start=True, stop=True), sig="te")
                p.w("vector", "te", xb)
                rv = p.do("vector", lambda e: e.tensor_copy(
                    ROWB, psum[2][0:1, 0:T]), sig="ve")
                bankw[2] = ("ve", rv)
                if is_real:
                    rv = p.do("vector", lambda e: e.tensor_copy(
                        ROWC, psum[3][0:1, 0:T]), sig="ve")
                    bankw[3] = ("ve", rv)
                if prev_read is not None:
                    p.w("tensor", "ve", prev_read)
                p.w("tensor", "ve", rv)
                bb = bcast(ROWB, 0, sig="te")
                if is_real:
                    bb = bcast(ROWC, 1, sig="te")
                p.w("vector", "te", bb)
                prev_st = None
                for c in range(ICH):
                    p.do("vector", lambda e, c=c: e.tensor_tensor(
                        SC(1), DT_(c), XC(c), ALU.mult))
                    p.do("vector", lambda e: e.tensor_tensor(
                        SC(1), SC(1), psum[0][:, 0:T], ALU.mult))
                    p.do("vector", lambda e, c=c, n=n: e.tensor_scalar_mul(
                        SC(0), DT_(c),
                        atile[:, c * 16 + n:c * 16 + n + 1]))
                    vt = p.do("vector", lambda e: e.engine_nop(), sig="ve")
                    p.w("scalar", "ve", vt)
                    st = p.do("scalar", lambda e: e.activation(
                        SC(2), SC(0), AF.Exp), sig="se")
                    p.w("vector", "se", st)
                    p.do("vector", lambda e, c=c, n=n: e.tensor_tensor_scan(
                        SC(0), SC(2), SC(1),
                        hstate[:, c * 16 + n:c * 16 + n + 1],
                        ALU.mult, ALU.add))
                    p.do("vector", lambda e: e.drain())
                    last = p.do("vector", lambda e, c=c, n=n: e.tensor_copy(
                        hstate[:, c * 16 + n:c * 16 + n + 1],
                        SC(0)[:, 511:512]),
                        sig="ve" if (not is_real and c == ICH - 1) else None)
                    if is_real:
                        if n == 0:
                            last = p.do("vector", lambda e, c=c:
                                        e.tensor_tensor(
                                            YF(c), SC(0), psum[1][:, 0:T],
                                            ALU.mult),
                                        sig="ve" if c == ICH - 1 else None)
                        else:
                            p.do("vector", lambda e: e.tensor_tensor(
                                SC(2), SC(0), psum[1][:, 0:T], ALU.mult))
                            last = p.do("vector", lambda e, c=c:
                                        e.tensor_tensor(
                                            YF(c), YF(c), SC(2), ALU.add),
                                        sig="ve" if c == ICH - 1 else None)
                prev_read = last
            if not is_real:
                return
            fence_sync(gpsimd=True)
            prev_y2mm = None
            prev_sil = None
            zkey = ("zi", 0)
            for c in range(ICH):
                if zkey in wpend:
                    csem, ctok = wpend[zkey]
                    p.w("gpsimd", csem, ctok)
                ztok = p.do("gpsimd", lambda e, c=c, rt=rt:
                            e.dma_start(out=ZINo[:],
                                        in_=dr["zscr"][rt, c, :, :]),
                            sig="zia", inc=16)
                p.w("scalar", "zia", ztok)
                if prev_sil is not None:
                    p.w("scalar", "ve", prev_sil)
                st = p.do("scalar", lambda e: e.activation(
                    ZSo[:], ZINo[:], AF.Silu), sig="se")
                wpend[zkey] = ("se", st)
                if prev_y2mm is not None:
                    p.w("vector", "te", prev_y2mm)
                p.do("vector", lambda e, c=c: e.scalar_tensor_tensor(
                    Y2A[:], XC(c), dvt[:, c:c + 1], YF(c),
                    ALU.mult, ALU.add))
                p.w("vector", "se", st)
                yv = prev_sil = p.do("vector", lambda e: e.tensor_tensor(
                    Y2B[:], Y2A[:], ZSo[:], ALU.mult), sig="ve")
                par, sem, tok, key = stream(
                    "op", ("wopa", "wopb"), WSO,
                    lambda c=c: dr["wop"][c, :, :])
                p.w("tensor", sem, tok)
                p.w("tensor", "ve", yv)
                lastev = None
                for m in range(8):
                    bank = 2 + (m % 6)
                    bank_wait("tensor", bank)
                    prev_y2mm = p.do(
                        "tensor", lambda e, m=m, b=bank, par=par: e.matmul(
                            psum[b][:, 0:T],
                            WSO(par)[:, m * 128:(m + 1) * 128],
                            Y2B[:], start=True, stop=True), sig="te")
                    p.w("vector", "te", prev_y2mm)
                    if c == 0:
                        lastev = p.do("vector", lambda e, m=m, b=bank:
                                      e.tensor_copy(MACC(m),
                                                    psum[b][:, 0:T]),
                                      sig="ve")
                    else:
                        lastev = p.do("vector", lambda e, m=m, b=bank:
                                      e.tensor_tensor(MACC(m), MACC(m),
                                                      psum[b][:, 0:T],
                                                      ALU.add), sig="ve")
                    bankw[bank] = ("ve", lastev)
                wpend[key] = ("te", prev_y2mm)
            p.w("scalar", "ve", lastev)
            lasts = None
            for m in range(8):
                lasts = p.do("scalar", lambda e, m=m: e.activation(
                    MT(m), MACC(m), AF.Gelu), sig="se")
            p.w("tensor", "se", lasts)
            sig_tok = None
            for half in range(2):
                gl = None
                for k in range(8):
                    par, sem, tok, key = stream(
                        "op", ("wopa", "wopb"), WSO,
                        lambda k=k, half=half: dr["wglu"][
                            k, :, half * 1024:(half + 1) * 1024])
                    p.w("tensor", sem, tok)
                    if k == 0:
                        for b in range(8):
                            bank_wait("tensor", b)
                    for ob in range(8):
                        gl = mm(WSO(par)[:, ob * 128:(ob + 1) * 128],
                                MT(k), ob, k == 0, k == 7,
                                sig="te" if ob == 7 else None)
                    wpend[key] = ("te", gl)
                if half == 0:
                    p.w("vector", "te", gl)
                    lastv = None
                    for ob in range(8):
                        lastv = p.do("vector", lambda e, ob=ob:
                                     e.tensor_scalar_add(
                                         G1(ob), psum[ob][:, 0:T],
                                         glubt[:, ob:ob + 1]), sig="ve")
                    for ob in range(8):
                        bankw[ob] = ("ve", lastv)
                else:
                    p.w("scalar", "te", gl)
                    lasts = None
                    for ob in range(8):
                        lasts = p.do("scalar", lambda e, ob=ob:
                                     e.activation(SG(ob), psum[ob][:, 0:T],
                                                  AF.Sigmoid,
                                                  bias=glubt[:, 8 + ob:
                                                             9 + ob]),
                                     sig="se")
                    sig_tok = lasts
                    for ob in range(8):
                        bankw[ob] = ("se", lasts)
            p.w("vector", "se", sig_tok)
            bank_wait("tensor", 6)
            bank_wait("tensor", 7)
            bcast(mu1, 6)
            br1 = bcast(r1, 7, sig="te")
            p.w("vector", "te", br1)
            lastv = None
            for c in range(DCH):
                p.do("vector", lambda e, c=c: e.tensor_tensor(
                    TMP[:], G1(c), SG(c), ALU.mult))
                p.do("vector", lambda e, c=c, x1c=x1c: e.tensor_tensor(
                    XN2S[:], x1c(c), psum[6][:, 0:T], ALU.subtract))
                p.do("vector", lambda e: e.tensor_tensor(
                    XN2S[:], XN2S[:], psum[7][:, 0:T], ALU.mult))
                p.do("vector", lambda e: e.tensor_tensor(
                    TMP[:], TMP[:], XN2S[:], ALU.add))
                lastv = p.do("vector", lambda e, c=c, x1c=x1c:
                             e.tensor_tensor(x1c(c), x1c(c), TMP[:],
                                             ALU.add), sig="ve")
            bankw[6] = ("ve", lastv)
            bankw[7] = ("ve", lastv)

        p.w("vector", "sx", consts_tok)
        p.w("tensor", "sx", consts_tok)
        p.w("tensor", "se", se_a)
        p.w("vector", "se", se_a)
        p.w("gpsimd", "sx", consts_tok)

        def load_x1(ti, is_real):
            rt = ti - wtiles
            if is_real:
                dst = x1res[:].rearrange("p (c s) -> p c s", c=DCH)[
                    :, :, rt * T:(rt + 1) * T]
            else:
                dst = X1WF[:].rearrange("p (c s) -> p c s", c=DCH)
            src = dr["xs"].rearrange("(c p) s -> p c s", p=128)[
                :, :, ti * T:(ti + 1) * T]
            return p.dma("sync", "sx", dst, src)

        def x1w_of(ti):
            return lambda c: X1WF[:, c * T:(c + 1) * T]

        def x1r_of(ti, rt=None):
            rt = ti - wtiles
            return lambda c, rt=rt: x1res[:, c * real + rt * T:
                                          c * real + rt * T + T]

        do_mamba = stage in ("mamba", "full")
        do_mamba_w = stage in ("mamba0", "mamba", "full")
        for ti in range(wtiles):
            if ti > 0:
                for s in ("te", "ve", "se"):
                    p.w("sync", s, p.cnt[s])
            xt = load_x1(ti, False)
            for eng in ("vector", "tensor", "scalar"):
                p.w(eng, "sx", xt)
            emit_ffn([ti], x1w_of)
            if do_mamba_w:
                emit_mamba(ti, is_real=False)
        for reg in range(rtiles // 2):
            t0, t1 = wtiles + 2 * reg, wtiles + 2 * reg + 1
            xt = None
            for ti in (t0, t1):
                xt = load_x1(ti, True)
            for eng in ("vector", "tensor", "scalar"):
                p.w(eng, "sx", xt)
            emit_ffn([t0, t1], x1r_of)
            if do_mamba:
                emit_mamba(t0, is_real=True)
                emit_mamba(t1, is_real=True)
        if stage == "full":
            for reg in range(rtiles // 2):
                t0, t1 = wtiles + 2 * reg, wtiles + 2 * reg + 1
                emit_ffn([t0, t1], x1r_of)
        vfin = p.do("vector", lambda e: e.engine_nop(), sig="ve")
        p.w("gpsimd", "ve", vfin)
        otok = p.do("gpsimd", lambda e: e.dma_start(
            out=dr["yout"].rearrange("(c p) s -> p c s", p=128),
            in_=x1res[:].rearrange("p (c s) -> p c s", c=DCH)),
            sig="out", inc=16)
        p.w("gpsimd", "out", otok)

        sems = {}
        for name in p.cnt:
            sems[name] = es.enter_context(nc.semaphore(name))
        block = es.enter_context(nc.Block())
        for engname in B.ENGS:
            def run(eng, q=p.q[engname]):
                for f in q:
                    f(eng, sems)
            getattr(block, engname)(run)

    return nc, p.cnt


_WTILES, _RTILES = 5, 4


def kernel(**inputs):
    w = _stage_weights(inputs)
    spans = _stage_x(inputs["L0_attn_x"], _WTILES, _RTILES)
    nc, _ = build_program(_WTILES, _RTILES)
    in_maps = []
    for core in range(8):
        m = dict(w)
        m["xs"] = spans[core]
        in_maps.append(m)
    res = run_bass_kernel_spmd(nc, in_maps, core_ids=list(range(8)))
    out = np.empty((2, 8192, 1024), np.float32)
    for core in range(8):
        seq, s0 = core // 4, (core % 4) * 2048
        out[seq, s0:s0 + 2048, :] = res.results[core]["yout"].T
    return out


# version 03
# speedup vs baseline: 1.0889x; 1.0685x over previous
import numpy as np
from contextlib import ExitStack

import concourse.bass as bass
import concourse.mybir as mybir
from concourse.bass_utils import run_bass_kernel_spmd

F32 = mybir.dt.float32
AF = mybir.ActivationFunctionType
ALU = mybir.AluOpType
AX = mybir.AxisListType.X

T = 512
DM, DCH = 1024, 8
DI, ICH = 2048, 16
NS = 16
NE, HD, HCH = 8, 2048, 16
EPS = 1e-5



class B:

    ENGS = ("tensor", "vector", "scalar", "gpsimd", "sync")

    def __init__(self):
        self.q = {e: [] for e in self.ENGS}
        self.cnt = {}

    def sem(self, *names):
        for n in names:
            self.cnt[n] = 0

    def do(self, eng, fn, sig=None, inc=1):
        if sig is None:
            self.q[eng].append(lambda e, S, fn=fn: fn(e))
            return None
        self.cnt[sig] += inc
        tok = self.cnt[sig]

        if eng in ("vector", "scalar") and inc == 1:
            def g(e, S, fn=fn, sig=sig, inc=inc):
                fn(e)
                e.drain()
                e.nop().then_inc(S[sig], inc)
        else:
            def g(e, S, fn=fn, sig=sig, inc=inc):
                fn(e).then_inc(S[sig], inc)

        self.q[eng].append(g)
        return tok

    def w(self, eng, sem, tok):
        if tok:
            self.q[eng].append(
                lambda e, S, sem=sem, tok=tok: e.wait_ge(S[sem], tok))

    def dma(self, eng, sem, out, in_):
        return self.do(eng, lambda e, o=out, i=in_: e.dma_start(out=o, in_=i),
                       sig=sem, inc=16)




def _stage_weights(ins):
    f = np.float32

    def c(a):
        return np.ascontiguousarray(a)

    return dict(
        wip=c(np.asarray(ins["L0_attn_in_proj_w"], f)
              .reshape(32, 128, 8, 128).transpose(0, 2, 3, 1)),
        wxp=c(np.asarray(ins["L0_attn_x_proj_w"], f)
              .reshape(96, 16, 128).transpose(1, 2, 0)),
        wdt=c(np.asarray(ins["L0_attn_dt_proj_w"], f)
              .reshape(16, 128, 64).transpose(0, 2, 1)),
        wop=c(np.asarray(ins["L0_attn_out_proj_w"], f)
              .reshape(1024, 16, 128).transpose(1, 2, 0)),
        wglu=c(np.asarray(ins["L0_attn_glu_w"], f)
               .reshape(2048, 8, 128).transpose(1, 2, 0)),
        wg=c(np.asarray(ins["L0_ffn_Wg"], f).reshape(8, 128, 8)),
        w1=c(np.asarray(ins["L0_ffn_W1"], f).reshape(8, 8, 128, 2048)),
        w2=c(np.asarray(ins["L0_ffn_W2"], f).reshape(8, 16, 128, 1024)),
        cw=c(np.asarray(ins["L0_attn_conv_w"], f)
             .reshape(16, 128, 4).transpose(1, 0, 2).reshape(128, 64)),
        cb=c(np.asarray(ins["L0_attn_conv_b"], f).reshape(16, 128).T),
        alog=c(np.asarray(ins["L0_attn_A_log"], f)
               .reshape(16, 128, 16).transpose(1, 0, 2).reshape(128, 256)),
        dvec=c(np.asarray(ins["L0_attn_D"], f).reshape(16, 128).T),
        dtb=c(np.asarray(ins["L0_attn_dt_proj_b"], f).reshape(16, 128).T),
        b1=c(np.asarray(ins["L0_ffn_b1"], f)
             .reshape(8, 16, 128).transpose(2, 0, 1).reshape(128, 128)),
        b2mm=c(np.asarray(ins["L0_ffn_b2"], f)),
        lng=c(np.asarray(ins["L0_attn_ln_g"], f).reshape(8, 128).T),
        lnb=c(np.asarray(ins["L0_attn_ln_b"], f).reshape(8, 128).T),
        glub=c(np.asarray(ins["L0_attn_glu_b"], f).reshape(16, 128).T),
        ident=np.eye(128, dtype=f),
        selb=c(np.eye(96, dtype=f)[:, 64:80]),
        selc=c(np.eye(96, dtype=f)[:, 80:96]),
        d8=c(np.eye(8, dtype=f).reshape(1, 64)),
    )


def _stage_x(x, wtiles, rtiles):
    f = np.float32
    x = np.asarray(x, f)
    W, R = wtiles * T, rtiles * T
    spans = []
    for core in range(8):
        seq, s0 = core // 4, (core % 4) * 2048
        w0 = max(0, s0 - W)
        win = np.concatenate(
            [np.zeros((W - (s0 - w0), DM), f), x[seq, w0:s0, :]], axis=0)
        span = np.concatenate([win, x[seq, s0:s0 + R, :]], axis=0)
        spans.append(np.ascontiguousarray(span.T))
    return spans




def build_program(wtiles=5, rtiles=4, stage="full"):
    ntiles = wtiles + rtiles
    span = ntiles * T
    real = rtiles * T

    nc = bass.Bass("TRN2", target_bir_lowering=False,
                   detect_race_conditions=False)
    dr = {}
    for name, shape in [
            ("xs", [DM, span]), ("wip", [32, 8, 128, 128]),
            ("wxp", [16, 128, 96]), ("wdt", [16, 64, 128]),
            ("wop", [16, 128, 1024]), ("wglu", [8, 128, 2048]),
            ("wg", [8, 128, 8]), ("w1", [8, 8, 128, 2048]),
            ("w2", [8, 16, 128, 1024]), ("cw", [128, 64]), ("cb", [128, 16]),
            ("alog", [128, 256]), ("dvec", [128, 16]), ("dtb", [128, 16]),
            ("b1", [128, 128]), ("b2mm", [8, 1024]),
            ("lng", [128, 8]), ("lnb", [128, 8]), ("glub", [128, 16]),
            ("ident", [128, 128]), ("d8", [1, 64]),
            ("selb", [96, 16]), ("selc", [96, 16])]:
        dr[name] = nc.dram_tensor(name, shape, F32, kind="ExternalInput").ap()
    dr["zscr"] = nc.dram_tensor("zscr", [rtiles, ICH, 128, T], F32,
                                kind="Internal").ap()
    dr["yout"] = nc.dram_tensor("yout", [DM, real], F32,
                                kind="ExternalOutput").ap()
    import os as _os
    _tap = _os.environ.get("KTAP", "")
    if _tap:
        dr["dbg"] = nc.dram_tensor("dbg", [128, 16 * T], F32,
                                   kind="ExternalOutput").ap()

    p = B()
    p.sem("sx", "wq1a", "wq1b", "wq2a", "wq2b", "wipa", "wipb",
          "wopa", "wopb", "zoa", "zob", "zia", "zib", "out",
          "te", "ve", "se", "ge")

    es = ExitStack()
    with es:
        sb = {}

        def alloc(name, shape):
            sb[name] = es.enter_context(nc.sbuf_tensor(name, shape, F32))
            return sb[name]

        x1res = alloc("x1res", [128, DCH * real])
        hstate = alloc("hstate", [128, 256])
        atile = alloc("atile", [128, 256])
        xprojw = alloc("xprojw", [128, 16 * 96])
        dtprojw = alloc("dtprojw", [64, 16 * 128])
        wgw = alloc("wgw", [128, 64])
        identt = alloc("identt", [128, 128])
        stats = alloc("stats", [128, 2 * T])
        scorest = alloc("scorest", [128, 2 * T])
        cwt = alloc("cwt", [128, 64])
        cbt = alloc("cbt", [128, 16])
        dvt = alloc("dvt", [128, 16])
        dtbt = alloc("dtbt", [128, 16])
        b1t = alloc("b1t", [128, 128])
        b2mmt = alloc("b2mmt", [8, 1024])
        lngt = alloc("lngt", [128, 8])
        lnbt = alloc("lnbt", [128, 8])
        glubt = alloc("glubt", [128, 16])
        selone = alloc("selone", [8, 128])
        d8t = alloc("d8t", [1, 64])
        onesc = alloc("onesc", [128, 1])
        ones1 = alloc("ones1", [1, 128])
        xdbl = alloc("xdbl", [128, 512])
        carry = alloc("carry", [128, 48])
        selbt = alloc("selbt", [96, 16])
        selct = alloc("selct", [96, 16])
        AR = alloc("arena", [128, 26192])

        def ar(off, size):
            assert off + size <= 26192
            return AR[:, off:off + size]

        def xn_ch(i, c):
            return AR[:, i * 4096 + c * T: i * 4096 + (c + 1) * T]

        X1WF = ar(4096, 4096)
        HT = lambda j: ar(8240 + j * T, T)
        LOGT = AR[0:8, 8240:8240 + T]
        WQ1 = lambda par: ar(16432 + par * 2048, 2048)
        WQ2 = lambda par: ar(20528 + par * 1024, 1024)
        SSRC = ar(22576, T)
        TMP = ar(23088, T)
        TMP2 = ar(23600, T)
        SROW = lambda i: AR[0:1, 24112 + i * T: 24112 + (i + 1) * T]
        RS = lambda i: AR[:, 26160 + i: 26161 + i]

        XPC = lambda c: AR[:, c * 515:(c + 1) * 515]
        DT_ = lambda c: AR[:, c * 515: c * 515 + 512]
        XC = lambda c: ar(8240 + c * T, T)
        X1WM = ar(8240, 4096)
        XN3 = lambda c: ar(16432 + c * T, T)
        WSA = lambda par: ar(20528 + par * 1024, 1024)
        ZST = lambda par: ar(12336 + par * T, T)
        SC = lambda i: ar(16432 + i * T, T)
        YF = lambda c: ar(17968 + c * T, T)
        WSO = lambda par: ar(par * 1024, 1024)
        Y2A = ar(2048, T)
        Y2B = ar(2560, T)
        ZSo = ar(3072, T)
        ZINo = ar(3584, T)
        MACC = lambda m: ar(4096 + m * T, T)
        MT = lambda m: ar(8240 + m * T, T)
        G1 = lambda ob: ar(12336 + ob * T, T)
        SG = lambda ob: ar(16432 + ob * T, T)
        XN2S = ar(20528, T)

        psum = [es.enter_context(nc.psum_tensor(f"pb{i}", [128, T], F32))
                for i in range(8)]

        bankw = [None] * 8

        def bank_wait(eng, b):
            if bankw[b] is not None:
                sem, tok = bankw[b]
                p.w(eng, sem, tok)
                bankw[b] = None

        def mm(lhsT, rhs, bank, start, stop, sig=None, mrows=128):
            return p.do("tensor",
                        lambda e, o=psum[bank][0:mrows, 0:T], l=lhsT, r=rhs,
                        st=start, sp=stop:
                        e.matmul(o, l, r, start=st, stop=sp), sig=sig)

        def recip_safe(dst, srcap, k1, k2):
            vt = p.do("vector", lambda e: e.engine_nop(), sig="ve")
            p.w("scalar", "ve", vt)
            p.do("scalar", lambda e: e.activation(RS(k1), srcap, AF.Square))
            p.do("scalar", lambda e: e.activation(RS(k2), RS(k1), AF.Ln))
            st = p.do("scalar", lambda e: e.activation(
                RS(k1), RS(k2), AF.Exp, scale=-1.0), sig="se")
            p.w("vector", "se", st)
            p.do("vector", lambda e: e.tensor_tensor(
                dst, srcap, RS(k1), ALU.mult))
            p.do("vector", lambda e: e.drain())

        def bcast(row, bank, sig=None):
            return p.do("tensor", lambda e, r=row, b=bank:
                        e.matmul(psum[b][:, 0:T], ones1[:], r,
                                 start=True, stop=True), sig=sig)

        p.do("vector", lambda e: e.memset(AR[:, 0:13096], 0.0))
        for gi in range(30):
            sl = (slice(13096, 26192) if gi % 2 == 0 else slice(0, 13096))
            s2 = (slice(0, 13096) if gi % 2 == 0 else slice(13096, 26192))
            p.do("vector", lambda e, sl=sl, s2=s2: e.tensor_copy(
                AR[:, sl], AR[:, s2]))
        guard = p.do("vector", lambda e: e.engine_nop(), sig="ve")
        p.w("tensor", "ve", guard)
        p.w("scalar", "ve", guard)
        p.w("gpsimd", "ve", guard)
        p.do("vector", lambda e: e.memset(onesc[:], 1.0))
        p.do("vector", lambda e: e.memset(ones1[:], 1.0))
        p.do("vector", lambda e: e.memset(hstate[:], 0.0))
        p.do("vector", lambda e: e.memset(carry[:], 0.0))
        p.do("vector", lambda e: e.drain())

        for name, dst, resh in [
                ("wxp", xprojw, ("k p j -> p k j", "p (k j) -> p k j", 16)),
                ("wdt", dtprojw, ("m p q -> p m q", "p (m q) -> p m q", 16)),
                ("wg", wgw, ("k p j -> p k j", "p (k j) -> p k j", 8))]:
            s_pat, d_pat, kk = resh
            p.dma("sync", "sx", dst[:].rearrange(d_pat, **(
                {"k": kk} if "k" in d_pat.split("->")[1] else {"m": kk})),
                dr[name].rearrange(s_pat))
        for name, dst in [("ident", identt), ("cw", cwt), ("cb", cbt),
                          ("dvec", dvt), ("dtb", dtbt), ("b1", b1t),
                          ("b2mm", b2mmt), ("lng", lngt), ("lnb", lnbt),
                          ("glub", glubt), ("d8", d8t),
                          ("selb", selbt), ("selc", selct)]:
            p.dma("sync", "sx", dst[:], dr[name][:, :])
        p.w("sync", "ve", p.cnt["ve"])
        alog_tok = p.dma("sync", "sx", atile[:], dr["alog"][:, :])
        p.w("scalar", "sx", alog_tok)
        p.do("scalar", lambda e: e.activation(atile[:], atile[:], AF.Exp))
        se_a = p.do("scalar", lambda e: e.mul(atile[:], atile[:], -1.0),
                    sig="se")
        consts_tok = p.cnt["sx"]

        def ln_stats(xch, nch, mu_row, r_row, var_row):
            bank_wait("tensor", 7)
            p.w("scalar", "ve", p.cnt["ve"])
            prev_mm = None
            for c in range(nch):
                if prev_mm is not None:
                    p.w("scalar", "te", prev_mm)
                sq = p.do("scalar", lambda e, c=c: e.activation(
                    TMP, xch(c), AF.Square), sig="se")
                p.do("tensor", lambda e, c=c, n=nch: e.matmul(
                    psum[7][0:1, 0:T], onesc[:], xch(c),
                    start=(c == 0), stop=(c == n - 1)))
                p.w("tensor", "se", sq)
                prev_mm = p.do("tensor", lambda e, c=c, n=nch: e.matmul(
                    psum[7][32:33, 0:T], onesc[:], TMP[:],
                    start=(c == 0), stop=(c == n - 1)), sig="te")
            p.w("vector", "te", prev_mm)
            nf = float(nch * 128)
            p.do("vector", lambda e: e.tensor_scalar_mul(
                mu_row, psum[7][0:1, 0:T], 1.0 / nf))
            p.do("vector", lambda e: e.tensor_scalar_mul(
                var_row, psum[7][32:33, 0:T], 1.0 / nf))
            p.do("vector", lambda e: e.tensor_tensor(
                TMP[0:1, 0:T], mu_row, mu_row, ALU.mult))
            p.do("vector", lambda e: e.drain())
            p.do("vector", lambda e: e.tensor_tensor(
                var_row, var_row, TMP[0:1, 0:T], ALU.subtract))
            p.do("vector", lambda e: e.drain())
            ve_tok = p.do("vector", lambda e: e.tensor_scalar_add(
                TMP[0:1, 0:T], var_row, EPS), sig="ve")
            p.w("scalar", "ve", ve_tok)
            p.do("scalar", lambda e: e.activation(
                TMP2[0:1, 0:T], TMP[0:1, 0:T], AF.Ln))
            se_tok = p.do("scalar", lambda e: e.activation(
                r_row, TMP2[0:1, 0:T], AF.Exp, scale=-0.5), sig="se")
            return ve_tok, se_tok

        def router(xnch, tile_col):
            bank_wait("tensor", 5)
            bank_wait("tensor", 6)
            lt = None
            for k in range(DCH):
                lt = mm(wgw[:, k * 8:(k + 1) * 8], xnch(k), 5,
                        k == 0, k == DCH - 1,
                        sig="te" if k == DCH - 1 else None, mrows=8)
            p.w("vector", "te", lt)
            vt = p.do("vector", lambda e: e.tensor_copy(
                LOGT, psum[5][0:8, 0:T]), sig="ve")
            PT = TMP[:, 0:8]
            EX = TMP[:, 16:24]
            SG1 = TMP[:, 32:40]
            EQ1 = TMP[:, 48:56]
            PM = TMP[:, 64:72]
            SG2 = TMP[:, 80:88]
            EQ2 = TMP[:, 96:104]
            T4 = TMP[:, 112:116]
            T2 = TMP[:, 128:130]
            SC1 = TMP[:, 144:152]
            SC2 = TMP[:, 160:168]
            SCR = TMP[:, 176:184]

            def pv(fn):
                p.do("vector", fn)
                p.do("vector", lambda e: e.drain())

            def tree8(srcfn, dst, op):
                pv(lambda e, s=srcfn, op=op: e.tensor_tensor(
                    T4, s()[:, 0:4], s()[:, 4:8], op))
                pv(lambda e, op=op: e.tensor_tensor(
                    T2, T4[:, 0:2], T4[:, 2:4], op))
                pv(lambda e, d=dst, op=op: e.tensor_tensor(
                    d, T2[:, 0:1], T2[:, 1:2], op))

            ptv = vt
            for tg in range(4):
                sl = slice(tg * 128, (tg + 1) * 128)
                p.w("tensor", "ve", ptv)
                tt = p.do("tensor", lambda e, sl=sl: e.transpose(
                    psum[6][:, 0:8], LOGT[:, sl], identt[0:8, 0:8]),
                    sig="te")
                p.w("vector", "te", tt)
                ptv = p.do("vector", lambda e: e.tensor_copy(
                    PT, psum[6][:, 0:8]), sig="ve")
                tree8(lambda: PT, RS(0), ALU.max)
                v2 = p.do("vector", lambda e: e.tensor_scalar_mul(
                    RS(1), RS(0), -1.0), sig="ve")
                if _tap == "rs1" and tg == 3 and tile_col == T:
                    p.w("gpsimd", "ve", v2)
                    p.dma("gpsimd", "zob", dr["dbg"][0:128, 0:192],
                          TMP[:, 0:192])
                    zt = p.dma("gpsimd", "zob", dr["dbg"][0:128, 192:224],
                               AR[:, 26160:26160 + 32])
                    p.w("vector", "zob", zt)
                p.w("scalar", "ve", v2)
                st = p.do("scalar", lambda e: e.activation(
                    EX, PT, AF.Exp, bias=RS(1)), sig="se")
                p.w("vector", "se", st)
                tree8(lambda: EX, RS(4), ALU.max)
                v2 = p.do("vector", lambda e: e.tensor_scalar_mul(
                    RS(5), RS(4), -1.0), sig="ve")
                p.w("scalar", "ve", v2)
                st = p.do("scalar", lambda e: e.activation(
                    SG1, EX, AF.Sign, bias=RS(5)), sig="se")
                p.w("vector", "se", st)
                pv(lambda e: e.tensor_scalar_add(EQ1, SG1, 1.0))
                tree8(lambda: EQ1, RS(6), ALU.add)
                pv(lambda e: e.scalar_tensor_tensor(
                    PM, EQ1, -2.0, EX, ALU.mult, ALU.add))
                tree8(lambda: PM, RS(7), ALU.max)
                v2 = p.do("vector", lambda e: e.tensor_scalar_mul(
                    RS(8), RS(7), -1.0), sig="ve")
                p.w("scalar", "ve", v2)
                st = p.do("scalar", lambda e: e.activation(
                    SG2, PM, AF.Sign, bias=RS(8)), sig="se")
                p.w("vector", "se", st)
                pv(lambda e: e.tensor_scalar_add(EQ2, SG2, 1.0))
                tree8(lambda: EQ2, RS(9), ALU.add)
                pv(lambda e: e.tensor_scalar_add(RS(10), RS(6), -1.0))
                pv(lambda e: e.tensor_scalar_min(RS(11), RS(10), 1.0))
                pv(lambda e: e.tensor_tensor(
                    RS(26), RS(4), RS(7), ALU.add))
                pv(lambda e: e.tensor_scalar_add(RS(12), RS(26), 1e-15))
                recip_safe(RS(13), RS(12), 22, 23)
                recip_safe(RS(14), RS(6), 22, 23)
                recip_safe(RS(15), RS(9), 22, 23)
                pv(lambda e: e.tensor_scalar(
                    RS(16), RS(11), -1.0, 1.0, ALU.mult, ALU.add))
                pv(lambda e: e.tensor_tensor(RS(17), RS(4), RS(13), ALU.mult))
                pv(lambda e: e.tensor_tensor(
                    RS(18), RS(17), RS(16), ALU.mult))
                pv(lambda e: e.tensor_tensor(
                    RS(19), RS(11), RS(14), ALU.mult))
                pv(lambda e: e.tensor_tensor(
                    RS(2), RS(18), RS(19), ALU.add))
                pv(lambda e: e.tensor_tensor(RS(20), RS(7), RS(13), ALU.mult))
                pv(lambda e: e.tensor_tensor(
                    RS(21), RS(20), RS(16), ALU.mult))
                pv(lambda e: e.tensor_tensor(
                    RS(3), RS(21), RS(15), ALU.mult))
                pv(lambda e: e.tensor_scalar_mul(RS(24), RS(2), 0.5))
                v2 = p.do("vector", lambda e: e.tensor_scalar_mul(
                    RS(25), RS(3), 0.5), sig="ve")
                pv(lambda e: e.tensor_scalar_mul(SC1, EQ1, RS(24)))
                pv(lambda e: e.tensor_scalar_mul(SC2, EQ2, RS(25)))
                v2 = p.do("vector", lambda e: e.tensor_tensor(
                    SCR, SC1, SC2, ALU.add), sig="ve")
                p.w("tensor", "ve", v2)
                bank_wait("tensor", 7)
                tt = p.do("tensor", lambda e: e.transpose(
                    psum[7][0:8, 0:128], SCR, identt[:]), sig="te")
                p.w("vector", "te", tt)
                scv = p.do("vector", lambda e, tg=tg, tc=tile_col:
                           e.tensor_copy(
                               scorest[0:8, tc + tg * 128:
                                       tc + (tg + 1) * 128],
                               psum[7][0:8, 0:128]), sig="ve")
                bankw[7] = ("ve", scv)
            p.w("scalar", "te", tt)
            p.w("vector", "te", tt)
            if _tap == "scores":
                p.w("gpsimd", "ve", p.cnt["ve"])
                p.dma("gpsimd", "zob", dr["dbg"][0:8, 0:2 * T], scorest[0:8, :])
                p.dma("gpsimd", "zob", dr["dbg"][0:128, 2 * T:2 * T + 192],
                      TMP[:, 0:192])
                p.dma("gpsimd", "zob",
                      dr["dbg"][0:128, 2 * T + 192:2 * T + 224],
                      AR[:, 26160:26160 + 32])
            return p.do("vector", lambda e: e.engine_nop(), sig="ve")

        wpar = {"q1": 0, "q2": 0, "ip": 0, "op": 0}
        wpend = {}

        def stream(tag, sems, buf_of, src_of):
            par = wpar[tag]
            sem = sems[par]
            key = (tag, par)
            if key in wpend:
                csem, ctok = wpend[key]
                p.w("sync", csem, ctok)
            tok = p.dma("sync", sem, buf_of(par), src_of())
            wpar[tag] ^= 1
            return par, sem, tok, key

        def fence_sync(gpsimd=False):
            for s in ("te", "ve", "se"):
                p.w("sync", s, p.cnt[s])
                if gpsimd:
                    p.w("gpsimd", s, p.cnt[s])

        def emit_ffn(tiles, x1_of):
            fence_sync()
            routers_only = _os.environ.get("KFFN") == "ln"
            n_experts = int(_os.environ.get("KNE", str(NE)))
            for i, ti in enumerate(tiles):
                xch = x1_of(ti)
                vtk, stk = ln_stats(xch, DCH, SROW(0), SROW(2), SROW(1))
                p.w("tensor", "ve", vtk)
                p.w("tensor", "se", stk)
                bcast(SROW(0), 6)
                brs = bcast(SROW(2), 7, sig="te")
                p.w("vector", "te", brs)
                lastv = None
                for c in range(DCH):
                    p.do("vector", lambda e, i=i, c=c, xch=xch:
                         e.tensor_tensor(xn_ch(i, c), xch(c),
                                         psum[6][:, 0:T], ALU.subtract))
                    lastv = p.do("vector", lambda e, i=i, c=c:
                                 e.tensor_tensor(xn_ch(i, c), xn_ch(i, c),
                                                 psum[7][:, 0:T], ALU.mult),
                                 sig="ve" if c == DCH - 1 else None)
                p.w("tensor", "ve", lastv)
                router(lambda c, i=i: xn_ch(i, c), i * T)
            prev_sr = None
            for e_i in range(0 if routers_only else n_experts):
                bank_wait("tensor", 5)
                bt = p.do("tensor", lambda e, e_i=e_i: e.matmul(
                    psum[5][0:8, 0:128], d8t[0:1, e_i * 8:e_i * 8 + 8],
                    ones1[0:1, 0:128], start=True, stop=True), sig="te")
                p.w("vector", "te", bt)
                if prev_sr is not None:
                    p.w("vector", "te", prev_sr)
                sv = p.do("vector", lambda e: e.tensor_copy(
                    selone[:], psum[5][0:8, 0:128]), sig="ve")
                p.w("tensor", "ve", sv)
                for i, ti in enumerate(tiles):
                    x1c = x1_of(ti)
                    bank_wait("tensor", 5)
                    sr = prev_sr = p.do(
                        "tensor", lambda e, e_i=e_i, i=i: e.matmul(
                            psum[5][:, 0:T],
                            selone[0:8, 0:128],
                            scorest[0:8, i * T:(i + 1) * T],
                            start=True, stop=True), sig="te")
                    p.w("vector", "te", sr)
                    srv = p.do("vector", lambda e: e.tensor_copy(
                        SSRC[:], psum[5][:, 0:T]), sig="ve")
                    bankw[5] = ("ve", srv)
                    if _os.environ.get("KFFN") == "srep":
                        continue
                    for oc in range(8):
                        par, sem, tok, key = stream(
                            "q1", ("wq1a", "wq1b"),
                            lambda par: WQ1(par).rearrange(
                                "p (k j) -> p k j", k=8),
                            lambda e_i=e_i, oc=oc: dr["w1"][
                                e_i, :, :, oc * 256:(oc + 1) * 256]
                            .rearrange("k p j -> p k j"))
                        p.w("tensor", sem, tok)
                        if (_tap == "wq1" and e_i == 0 and i == 0
                                and oc == 0):
                            p.w("gpsimd", sem, tok)
                            dtp = p.dma("gpsimd", "zob", dr["dbg"][:, 0:2048],
                                        WQ1(par))
                            p.w("tensor", "zob", dtp)
                        lastmm = None
                        for jj in range(2):
                            j = oc * 2 + jj
                            bank = j % 4
                            bank_wait("tensor", bank)
                            for k in range(8):
                                lastmm = mm(
                                    WQ1(par)[:, (k * 2 + jj) * 128:
                                             (k * 2 + jj + 1) * 128],
                                    xn_ch(i, k), bank, k == 0, k == 7,
                                    sig="te" if k == 7 else None)
                            p.w("scalar", "te", lastmm)
                            st = p.do("scalar",
                                      lambda e, j=j, b=bank, ei=e_i:
                                      e.activation(
                                          HT(j), psum[b][:, 0:T], AF.Relu,
                                          bias=b1t[:, ei * 16 + j:
                                                   ei * 16 + j + 1]),
                                      sig="se")
                            bankw[bank] = ("se", st)
                        wpend[key] = ("te", lastmm)
                    for b in range(4):
                        bank_wait("tensor", b)
                    if _os.environ.get("KFFN") == "h":
                        continue
                    if _tap == "h00" and e_i == 0 and i == 0:
                        p.w("gpsimd", "se", p.cnt["se"])
                        dtp = p.dma("gpsimd", "zob", dr["dbg"][:, :],
                                    AR[:, 8240:8240 + 16 * T])
                        p.w("tensor", "zob", dtp)
                    p.w("tensor", "ve", srv)
                    lastmm = None
                    for k in range(16):
                        par, sem, tok, key = stream(
                            "q2", ("wq2a", "wq2b"), WQ2,
                            lambda e_i=e_i, k=k: dr["w2"][e_i, k, :, :])
                        p.w("tensor", sem, tok)
                        for m in range(8):
                            lastmm = mm(WQ2(par)[:, m * 128:(m + 1) * 128],
                                        HT(k), m, k == 0, k == 15,
                                        sig="te" if m == 7 else None)
                        wpend[key] = ("te", lastmm)
                    p.w("vector", "te", lastmm)
                    lastv = None
                    for m in range(8):
                        p.do("vector", lambda e, m=m: e.tensor_tensor(
                            TMP[:], psum[m][:, 0:T], SSRC[:], ALU.mult))
                        lastv = p.do("vector",
                                     lambda e, m=m, x1c=x1c:
                                     e.tensor_tensor(x1c(m), x1c(m), TMP[:],
                                                     ALU.add), sig="ve")
                    for m in range(8):
                        bankw[m] = ("ve", lastv)
                    for m in range(8):
                        bank_wait("tensor", m)
            skip_b2 = routers_only or _os.environ.get("KB2") == "0"
            for i, ti in enumerate(tiles) if not skip_b2 else []:
                x1c = x1_of(ti)
                lt = None
                for m in range(8):
                    bank_wait("tensor", m)
                    lt = p.do("tensor", lambda e, m=m, i=i: e.matmul(
                        psum[m][:, 0:T],
                        b2mmt[0:8, m * 128:(m + 1) * 128],
                        scorest[0:8, i * T:(i + 1) * T],
                        start=True, stop=True),
                        sig="te" if m == 7 else None)
                p.w("vector", "te", lt)
                lastv = None
                for m in range(8):
                    lastv = p.do("vector", lambda e, m=m, x1c=x1c:
                                 e.tensor_tensor(x1c(m), x1c(m),
                                                 psum[m][:, 0:T], ALU.add),
                                 sig="ve")
                for m in range(8):
                    bankw[m] = ("ve", lastv)

        _mph = int(_os.environ.get("KMAMPH", "99"))
        wcopy_tok = [None]

        def emit_mamba(ti, is_real):
            rt = ti - wtiles
            if is_real:
                def x1c(c, rt=rt):
                    return x1res[:, c * real + rt * T:
                                 c * real + rt * T + T]
                mu1 = stats[0:1, 0:T]
                r1 = stats[0:1, T:2 * T]
            else:
                p.w("vector", "se", p.cnt["se"])
                lastv = None
                for c in range(DCH):
                    lastv = p.do("vector", lambda e, c=c: e.tensor_copy(
                        X1WM[:, c * T:(c + 1) * T],
                        X1WF[:, c * T:(c + 1) * T]), sig="ve")
                p.w("tensor", "ve", lastv)
                p.w("scalar", "ve", lastv)
                wcopy_tok[0] = lastv

                def x1c(c):
                    return X1WM[:, c * T:(c + 1) * T]
                mu1, r1 = SROW(0), SROW(2)
            vtk, stk = ln_stats(x1c, DCH, mu1, r1, SROW(1))
            p.w("vector", "se", stk)
            p.do("vector", lambda e: e.tensor_tensor(
                SROW(3), r1, r1, ALU.mult))
            p.do("vector", lambda e: e.drain())
            p.do("vector", lambda e: e.tensor_tensor(
                SROW(3), SROW(3), SROW(1), ALU.mult))
            p.do("vector", lambda e: e.drain())
            vt = p.do("vector", lambda e: e.tensor_scalar_add(
                SROW(3), SROW(3), EPS), sig="ve")
            p.w("scalar", "ve", vt)
            p.do("scalar", lambda e: e.activation(SROW(3), SROW(3), AF.Ln))
            st = p.do("scalar", lambda e: e.activation(
                SROW(3), SROW(3), AF.Exp, scale=-0.5), sig="se")
            p.w("vector", "se", st)
            vt = p.do("vector", lambda e: e.tensor_tensor(
                SROW(3), SROW(3), r1, ALU.mult), sig="ve")
            p.w("tensor", "ve", vtk)
            p.w("tensor", "ve", vt)
            bcast(mu1, 6)
            bs3 = bcast(SROW(3), 7, sig="te")
            p.w("vector", "te", bs3)
            lastv = None
            for c in range(DCH):
                p.do("vector", lambda e, c=c, x1c=x1c: e.tensor_tensor(
                    XN3(c), x1c(c), psum[6][:, 0:T], ALU.subtract))
                p.do("vector", lambda e, c=c: e.tensor_tensor(
                    XN3(c), XN3(c), psum[7][:, 0:T], ALU.mult))
                lastv = p.do("vector", lambda e, c=c: e.tensor_scalar(
                    XN3(c), XN3(c), lngt[:, c:c + 1], lnbt[:, c:c + 1],
                    ALU.mult, ALU.add), sig="ve")
            p.w("tensor", "ve", lastv)
            lastc = None
            for c in range(ICH):
                lastc = p.do("vector", lambda e, c=c: e.tensor_copy(
                    AR[:, c * 515:c * 515 + 3],
                    carry[:, c * 3:(c + 1) * 3]), sig="ve")
            p.w("tensor", "ve", lastc)
            if not is_real and _mph < 2:
                return
            fence_sync()
            nm = 32 if is_real else 16
            for m_i in range(nm):
                par, sem, tok, key = stream(
                    "ip", ("wipa", "wipb"),
                    lambda par: WSA(par)[:].rearrange(
                        "p (k q) -> p k q", k=8),
                    lambda m_i=m_i: dr["wip"][m_i].rearrange(
                        "k p q -> p k q"))
                p.w("tensor", sem, tok)
                bank = m_i % 2
                bank_wait("tensor", bank)
                lt = None
                for k in range(8):
                    lt = mm(WSA(par)[:, k * 128:(k + 1) * 128], XN3(k),
                            bank, k == 0, k == 7,
                            sig="te" if k == 7 else None)
                wpend[key] = ("te", lt)
                if m_i < 16:
                    p.w("vector", "te", lt)
                    ev = p.do("vector", lambda e, m_i=m_i, b=bank:
                              e.tensor_copy(
                                  AR[:, m_i * 515 + 3:m_i * 515 + 515],
                                  psum[b][:, 0:T]), sig="ve")
                    bankw[bank] = ("ve", ev)
                else:
                    p.w("vector", "te", lt)
                    zsem = "zoa" if bank == 0 else "zob"
                    zkey = ("zo", bank)
                    if zkey in wpend:
                        csem, ctok = wpend[zkey]
                        p.w("vector", csem, ctok)
                    ev = p.do("vector", lambda e, b=bank: e.tensor_copy(
                        ZST(b)[:], psum[b][:, 0:T]), sig="ve")
                    bankw[bank] = ("ve", ev)
                    p.w("gpsimd", "ve", ev)
                    ztok = p.do("gpsimd",
                                lambda e, m_i=m_i, b=bank, rt=rt:
                                e.dma_start(out=dr["zscr"][rt, m_i - 16, :, :],
                                            in_=ZST(b)[:]),
                                sig=zsem, inc=16)
                    wpend[zkey] = (zsem, ztok)
            bank_wait("vector", 0)
            bank_wait("vector", 1)
            if not is_real and _mph < 3:
                return
            lasts = None
            prev_sil = None
            for c in range(ICH):
                if prev_sil is not None:
                    p.w("vector", "se", prev_sil)
                p.do("vector", lambda e, c=c: e.tensor_scalar_mul(
                    TMP[:], XPC(c)[:, 0:512], cwt[:, c * 4:c * 4 + 1]))
                for k in range(1, 4):
                    p.do("vector", lambda e, c=c, k=k:
                         e.scalar_tensor_tensor(
                             TMP[:], XPC(c)[:, k:k + 512],
                             cwt[:, c * 4 + k:c * 4 + k + 1], TMP[:],
                             ALU.mult, ALU.add))
                vt = p.do("vector", lambda e: e.engine_nop(), sig="ve")
                p.w("scalar", "ve", vt)
                lasts = prev_sil = p.do("scalar", lambda e, c=c: e.activation(
                    XC(c), TMP[:], AF.Silu, bias=cbt[:, c:c + 1]), sig="se")
            for c in range(ICH):
                p.do("vector", lambda e, c=c: e.tensor_copy(
                    carry[:, c * 3:(c + 1) * 3],
                    AR[:, c * 515 + 512:c * 515 + 515]))
                p.do("vector", lambda e: e.drain())
            p.w("tensor", "se", lasts)
            bank_wait("tensor", 2)
            lt = None
            for c in range(ICH):
                lt = p.do("tensor", lambda e, c=c: e.matmul(
                    psum[2][0:96, 0:T], xprojw[:, c * 96:(c + 1) * 96],
                    XC(c), start=(c == 0), stop=(c == ICH - 1)),
                    sig="te" if c == ICH - 1 else None)
            p.w("vector", "te", lt)
            vt = p.do("vector", lambda e: e.tensor_copy(
                xdbl[0:96, :], psum[2][0:96, 0:T]), sig="ve")
            p.w("tensor", "ve", vt)
            p.w("scalar", "ve", vt)
            p.w("gpsimd", "ve", vt)
            bankw[2] = None
            if not is_real and _mph < 4:
                return
            lastd = None
            for m_i in range(ICH):
                bank = m_i % 2
                bank_wait("tensor", bank)
                lt = p.do("tensor", lambda e, m_i=m_i, b=bank: e.matmul(
                    psum[b][:, 0:T],
                    dtprojw[:, m_i * 128:(m_i + 1) * 128],
                    xdbl[0:64, :], start=True, stop=True), sig="te")
                p.w("scalar", "te", lt)
                p.do("scalar", lambda e, m_i=m_i, b=bank: e.activation(
                    TMP2[:], psum[b][:, 0:T], AF.Exp,
                    bias=dtbt[:, m_i:m_i + 1]))
                lastd = p.do("scalar", lambda e, m_i=m_i: e.activation(
                    DT_(m_i), TMP2[:], AF.Ln, bias=1.0), sig="se")
                bankw[bank] = ("se", lastd)
            p.w("vector", "se", lastd)
            p.w("tensor", "se", lastd)
            bankw[0] = bankw[1] = None
            if not is_real and _mph < 5:
                return
            ROWB = SC(0)[0:1, 0:T]
            ROWC = SC(1)[0:1, 0:T]
            prev_read = None
            for n in range(NS):
                bank_wait("tensor", 2)
                xb = p.do("tensor", lambda e, n=n: e.matmul(
                    psum[2][0:1, 0:T], selbt[0:96, n:n + 1], xdbl[0:96, :],
                    start=True, stop=True), sig="te")
                if is_real:
                    bank_wait("tensor", 3)
                    xb = p.do("tensor", lambda e, n=n: e.matmul(
                        psum[3][0:1, 0:T], selct[0:96, n:n + 1],
                        xdbl[0:96, :], start=True, stop=True), sig="te")
                p.w("vector", "te", xb)
                rv = p.do("vector", lambda e: e.tensor_copy(
                    ROWB, psum[2][0:1, 0:T]), sig="ve")
                bankw[2] = ("ve", rv)
                if is_real:
                    rv = p.do("vector", lambda e: e.tensor_copy(
                        ROWC, psum[3][0:1, 0:T]), sig="ve")
                    bankw[3] = ("ve", rv)
                if prev_read is not None:
                    p.w("tensor", "ve", prev_read)
                p.w("tensor", "ve", rv)
                bb = bcast(ROWB, 0, sig="te")
                if is_real:
                    bb = bcast(ROWC, 1, sig="te")
                p.w("vector", "te", bb)
                prev_st = None
                for c in range(ICH):
                    p.do("vector", lambda e, c=c: e.tensor_tensor(
                        SC(1), DT_(c), XC(c), ALU.mult))
                    p.do("vector", lambda e: e.tensor_tensor(
                        SC(1), SC(1), psum[0][:, 0:T], ALU.mult))
                    p.do("vector", lambda e, c=c, n=n: e.tensor_scalar_mul(
                        SC(0), DT_(c),
                        atile[:, c * 16 + n:c * 16 + n + 1]))
                    vt = p.do("vector", lambda e: e.engine_nop(), sig="ve")
                    p.w("scalar", "ve", vt)
                    st = p.do("scalar", lambda e: e.activation(
                        SC(2), SC(0), AF.Exp), sig="se")
                    p.w("vector", "se", st)
                    p.do("vector", lambda e, c=c, n=n: e.tensor_tensor_scan(
                        SC(0), SC(2), SC(1),
                        hstate[:, c * 16 + n:c * 16 + n + 1],
                        ALU.mult, ALU.add))
                    p.do("vector", lambda e: e.drain())
                    last = p.do("vector", lambda e, c=c, n=n: e.tensor_copy(
                        hstate[:, c * 16 + n:c * 16 + n + 1],
                        SC(0)[:, 511:512]),
                        sig="ve" if (not is_real and c == ICH - 1) else None)
                    if is_real:
                        if n == 0:
                            last = p.do("vector", lambda e, c=c:
                                        e.tensor_tensor(
                                            YF(c), SC(0), psum[1][:, 0:T],
                                            ALU.mult),
                                        sig="ve" if c == ICH - 1 else None)
                        else:
                            p.do("vector", lambda e: e.tensor_tensor(
                                SC(2), SC(0), psum[1][:, 0:T], ALU.mult))
                            last = p.do("vector", lambda e, c=c:
                                        e.tensor_tensor(
                                            YF(c), YF(c), SC(2), ALU.add),
                                        sig="ve" if c == ICH - 1 else None)
                prev_read = last
            if not is_real:
                return
            fence_sync(gpsimd=True)
            prev_y2mm = None
            prev_sil = None
            zkey = ("zi", 0)
            for c in range(ICH):
                if zkey in wpend:
                    csem, ctok = wpend[zkey]
                    p.w("gpsimd", csem, ctok)
                ztok = p.do("gpsimd", lambda e, c=c, rt=rt:
                            e.dma_start(out=ZINo[:],
                                        in_=dr["zscr"][rt, c, :, :]),
                            sig="zia", inc=16)
                p.w("scalar", "zia", ztok)
                if prev_sil is not None:
                    p.w("scalar", "ve", prev_sil)
                st = p.do("scalar", lambda e: e.activation(
                    ZSo[:], ZINo[:], AF.Silu), sig="se")
                wpend[zkey] = ("se", st)
                if prev_y2mm is not None:
                    p.w("vector", "te", prev_y2mm)
                p.do("vector", lambda e, c=c: e.scalar_tensor_tensor(
                    Y2A[:], XC(c), dvt[:, c:c + 1], YF(c),
                    ALU.mult, ALU.add))
                p.w("vector", "se", st)
                yv = prev_sil = p.do("vector", lambda e: e.tensor_tensor(
                    Y2B[:], Y2A[:], ZSo[:], ALU.mult), sig="ve")
                par, sem, tok, key = stream(
                    "op", ("wopa", "wopb"), WSO,
                    lambda c=c: dr["wop"][c, :, :])
                p.w("tensor", sem, tok)
                p.w("tensor", "ve", yv)
                lastev = None
                for m in range(8):
                    bank = 2 + (m % 6)
                    bank_wait("tensor", bank)
                    prev_y2mm = p.do(
                        "tensor", lambda e, m=m, b=bank, par=par: e.matmul(
                            psum[b][:, 0:T],
                            WSO(par)[:, m * 128:(m + 1) * 128],
                            Y2B[:], start=True, stop=True), sig="te")
                    p.w("vector", "te", prev_y2mm)
                    if c == 0:
                        lastev = p.do("vector", lambda e, m=m, b=bank:
                                      e.tensor_copy(MACC(m),
                                                    psum[b][:, 0:T]),
                                      sig="ve")
                    else:
                        lastev = p.do("vector", lambda e, m=m, b=bank:
                                      e.tensor_tensor(MACC(m), MACC(m),
                                                      psum[b][:, 0:T],
                                                      ALU.add), sig="ve")
                    bankw[bank] = ("ve", lastev)
                wpend[key] = ("te", prev_y2mm)
            p.w("scalar", "ve", lastev)
            lasts = None
            for m in range(8):
                lasts = p.do("scalar", lambda e, m=m: e.activation(
                    MT(m), MACC(m), AF.Gelu), sig="se")
            p.w("tensor", "se", lasts)
            sig_tok = None
            for half in range(2):
                gl = None
                for k in range(8):
                    par, sem, tok, key = stream(
                        "op", ("wopa", "wopb"), WSO,
                        lambda k=k, half=half: dr["wglu"][
                            k, :, half * 1024:(half + 1) * 1024])
                    p.w("tensor", sem, tok)
                    if k == 0:
                        for b in range(8):
                            bank_wait("tensor", b)
                    for ob in range(8):
                        gl = mm(WSO(par)[:, ob * 128:(ob + 1) * 128],
                                MT(k), ob, k == 0, k == 7,
                                sig="te" if ob == 7 else None)
                    wpend[key] = ("te", gl)
                if half == 0:
                    p.w("vector", "te", gl)
                    lastv = None
                    for ob in range(8):
                        lastv = p.do("vector", lambda e, ob=ob:
                                     e.tensor_scalar_add(
                                         G1(ob), psum[ob][:, 0:T],
                                         glubt[:, ob:ob + 1]), sig="ve")
                    for ob in range(8):
                        bankw[ob] = ("ve", lastv)
                else:
                    p.w("scalar", "te", gl)
                    lasts = None
                    for ob in range(8):
                        lasts = p.do("scalar", lambda e, ob=ob:
                                     e.activation(SG(ob), psum[ob][:, 0:T],
                                                  AF.Sigmoid,
                                                  bias=glubt[:, 8 + ob:
                                                             9 + ob]),
                                     sig="se")
                    sig_tok = lasts
                    for ob in range(8):
                        bankw[ob] = ("se", lasts)
            p.w("vector", "se", sig_tok)
            bank_wait("tensor", 6)
            bank_wait("tensor", 7)
            bcast(mu1, 6)
            br1 = bcast(r1, 7, sig="te")
            p.w("vector", "te", br1)
            lastv = None
            for c in range(DCH):
                p.do("vector", lambda e, c=c: e.tensor_tensor(
                    TMP[:], G1(c), SG(c), ALU.mult))
                p.do("vector", lambda e, c=c, x1c=x1c: e.tensor_tensor(
                    XN2S[:], x1c(c), psum[6][:, 0:T], ALU.subtract))
                p.do("vector", lambda e: e.tensor_tensor(
                    XN2S[:], XN2S[:], psum[7][:, 0:T], ALU.mult))
                p.do("vector", lambda e: e.tensor_tensor(
                    TMP[:], TMP[:], XN2S[:], ALU.add))
                lastv = p.do("vector", lambda e, c=c, x1c=x1c:
                             e.tensor_tensor(x1c(c), x1c(c), TMP[:],
                                             ALU.add), sig="ve")
            bankw[6] = ("ve", lastv)
            bankw[7] = ("ve", lastv)

        p.w("vector", "sx", consts_tok)
        p.w("tensor", "sx", consts_tok)
        p.w("tensor", "se", se_a)
        p.w("vector", "se", se_a)
        p.w("gpsimd", "sx", consts_tok)

        def load_x1(ti, is_real):
            rt = ti - wtiles
            if is_real:
                dst = x1res[:].rearrange("p (c s) -> p c s", c=DCH)[
                    :, :, rt * T:(rt + 1) * T]
            else:
                dst = X1WF[:].rearrange("p (c s) -> p c s", c=DCH)
            src = dr["xs"].rearrange("(c p) s -> p c s", p=128)[
                :, :, ti * T:(ti + 1) * T]
            return p.dma("sync", "sx", dst, src)

        def x1w_of(ti):
            return lambda c: X1WF[:, c * T:(c + 1) * T]

        def x1r_of(ti, rt=None):
            rt = ti - wtiles
            return lambda c, rt=rt: x1res[:, c * real + rt * T:
                                          c * real + rt * T + T]

        do_mamba = stage in ("mamba", "full")
        do_mamba_w = stage in ("mamba0", "mamba", "full")
        for ti in range(wtiles):
            if ti > 0:
                for s in ("te", "ve", "se"):
                    p.w("sync", s, p.cnt[s])
            xt = load_x1(ti, False)
            for eng in ("vector", "tensor", "scalar"):
                p.w(eng, "sx", xt)
            emit_ffn([ti], x1w_of)
            if do_mamba_w:
                emit_mamba(ti, is_real=False)
        for reg in range(rtiles // 2):
            t0, t1 = wtiles + 2 * reg, wtiles + 2 * reg + 1
            xt = None
            for ti in (t0, t1):
                xt = load_x1(ti, True)
            for eng in ("vector", "tensor", "scalar"):
                p.w(eng, "sx", xt)
            emit_ffn([t0, t1], x1r_of)
            if do_mamba:
                emit_mamba(t0, is_real=True)
                emit_mamba(t1, is_real=True)
        if stage == "full":
            for reg in range(rtiles // 2):
                t0, t1 = wtiles + 2 * reg, wtiles + 2 * reg + 1
                emit_ffn([t0, t1], x1r_of)
        vfin = p.do("vector", lambda e: e.engine_nop(), sig="ve")
        p.w("gpsimd", "ve", vfin)
        otok = p.do("gpsimd", lambda e: e.dma_start(
            out=dr["yout"].rearrange("(c p) s -> p c s", p=128),
            in_=x1res[:].rearrange("p (c s) -> p c s", c=DCH)),
            sig="out", inc=16)
        p.w("gpsimd", "out", otok)

        sems = {}
        for name in p.cnt:
            sems[name] = es.enter_context(nc.semaphore(name))
        block = es.enter_context(nc.Block())
        for engname in B.ENGS:
            def run(eng, q=p.q[engname]):
                for f in q:
                    f(eng, sems)
            getattr(block, engname)(run)

    return nc, p.cnt


_WTILES, _RTILES = 4, 4


def kernel(**inputs):
    w = _stage_weights(inputs)
    spans = _stage_x(inputs["L0_attn_x"], _WTILES, _RTILES)
    nc, _ = build_program(_WTILES, _RTILES)
    in_maps = []
    for core in range(8):
        m = dict(w)
        m["xs"] = spans[core]
        in_maps.append(m)
    res = run_bass_kernel_spmd(nc, in_maps, core_ids=list(range(8)))
    out = np.empty((2, 8192, 1024), np.float32)
    for core in range(8):
        seq, s0 = core // 4, (core % 4) * 2048
        out[seq, s0:s0 + 2048, :] = res.results[core]["yout"].T
    return out


# version 04 (final)
# speedup vs baseline: 1.2092x; 1.1105x over previous
import numpy as np
from contextlib import ExitStack

import concourse.bass as bass
import concourse.mybir as mybir
from concourse.bass_utils import run_bass_kernel_spmd

F32 = mybir.dt.float32
AF = mybir.ActivationFunctionType
ALU = mybir.AluOpType
AX = mybir.AxisListType.X

T = 512
DM, DCH = 1024, 8
DI, ICH = 2048, 16
NS = 16
NE, HD, HCH = 8, 2048, 16
EPS = 1e-5



class B:

    ENGS = ("tensor", "vector", "scalar", "gpsimd", "sync")

    def __init__(self):
        self.q = {e: [] for e in self.ENGS}
        self.cnt = {}

    def sem(self, *names):
        for n in names:
            self.cnt[n] = 0

    def do(self, eng, fn, sig=None, inc=1):
        if sig is None:
            self.q[eng].append(lambda e, S, fn=fn: fn(e))
            return None
        self.cnt[sig] += inc
        tok = self.cnt[sig]

        if eng in ("vector", "scalar") and inc == 1:
            def g(e, S, fn=fn, sig=sig, inc=inc):
                fn(e)
                e.drain()
                e.nop().then_inc(S[sig], inc)
        else:
            def g(e, S, fn=fn, sig=sig, inc=inc):
                fn(e).then_inc(S[sig], inc)

        self.q[eng].append(g)
        return tok

    def w(self, eng, sem, tok):
        if tok:
            self.q[eng].append(
                lambda e, S, sem=sem, tok=tok: e.wait_ge(S[sem], tok))

    def dma(self, eng, sem, out, in_):
        return self.do(eng, lambda e, o=out, i=in_: e.dma_start(out=o, in_=i),
                       sig=sem, inc=16)




def _stage_weights(ins):
    f = np.float32

    def c(a):
        return np.ascontiguousarray(a)

    return dict(
        wip=c(np.asarray(ins["L0_attn_in_proj_w"], f)
              .reshape(32, 128, 8, 128).transpose(0, 2, 3, 1)),
        wxp=c(np.asarray(ins["L0_attn_x_proj_w"], f)
              .reshape(96, 16, 128).transpose(1, 2, 0)),
        wdt=c(np.asarray(ins["L0_attn_dt_proj_w"], f)
              .reshape(16, 128, 64).transpose(0, 2, 1)),
        wop=c(np.asarray(ins["L0_attn_out_proj_w"], f)
              .reshape(1024, 16, 128).transpose(1, 2, 0)),
        wglu=c(np.asarray(ins["L0_attn_glu_w"], f)
               .reshape(2048, 8, 128).transpose(1, 2, 0)),
        wg=c(np.asarray(ins["L0_ffn_Wg"], f).reshape(8, 128, 8)),
        w1=c(np.asarray(ins["L0_ffn_W1"], f).reshape(8, 8, 128, 2048)),
        w2=c(np.asarray(ins["L0_ffn_W2"], f).reshape(8, 16, 128, 1024)),
        cw=c(np.asarray(ins["L0_attn_conv_w"], f)
             .reshape(16, 128, 4).transpose(1, 0, 2).reshape(128, 64)),
        cb=c(np.asarray(ins["L0_attn_conv_b"], f).reshape(16, 128).T),
        alog=c(np.asarray(ins["L0_attn_A_log"], f)
               .reshape(16, 128, 16).transpose(1, 0, 2).reshape(128, 256)),
        dvec=c(np.asarray(ins["L0_attn_D"], f).reshape(16, 128).T),
        dtb=c(np.asarray(ins["L0_attn_dt_proj_b"], f).reshape(16, 128).T),
        b1=c(np.asarray(ins["L0_ffn_b1"], f)
             .reshape(8, 16, 128).transpose(2, 0, 1).reshape(128, 128)),
        b2mm=c(np.asarray(ins["L0_ffn_b2"], f)),
        lng=c(np.asarray(ins["L0_attn_ln_g"], f).reshape(8, 128).T),
        lnb=c(np.asarray(ins["L0_attn_ln_b"], f).reshape(8, 128).T),
        glub=c(np.asarray(ins["L0_attn_glu_b"], f).reshape(16, 128).T),
        ident=np.eye(128, dtype=f),
        selb=c(np.eye(96, dtype=f)[:, 64:80]),
        selc=c(np.eye(96, dtype=f)[:, 80:96]),
        d8=c(np.eye(8, dtype=f).reshape(1, 64)),
    )


def _stage_x(x, wtiles, rtiles):
    f = np.float32
    x = np.asarray(x, f)
    W, R = wtiles * T, rtiles * T
    spans = []
    for core in range(8):
        seq, s0 = core // 4, (core % 4) * 2048
        w0 = max(0, s0 - W)
        win = np.concatenate(
            [np.zeros((W - (s0 - w0), DM), f), x[seq, w0:s0, :]], axis=0)
        span = np.concatenate([win, x[seq, s0:s0 + R, :]], axis=0)
        spans.append(np.ascontiguousarray(span.T))
    return spans




def build_program(wtiles=5, rtiles=4, stage="full"):
    ntiles = wtiles + rtiles
    span = ntiles * T
    real = rtiles * T

    nc = bass.Bass("TRN2", target_bir_lowering=False,
                   detect_race_conditions=False)
    dr = {}
    for name, shape in [
            ("xs", [DM, span]), ("wip", [32, 8, 128, 128]),
            ("wxp", [16, 128, 96]), ("wdt", [16, 64, 128]),
            ("wop", [16, 128, 1024]), ("wglu", [8, 128, 2048]),
            ("wg", [8, 128, 8]), ("w1", [8, 8, 128, 2048]),
            ("w2", [8, 16, 128, 1024]), ("cw", [128, 64]), ("cb", [128, 16]),
            ("alog", [128, 256]), ("dvec", [128, 16]), ("dtb", [128, 16]),
            ("b1", [128, 128]), ("b2mm", [8, 1024]),
            ("lng", [128, 8]), ("lnb", [128, 8]), ("glub", [128, 16]),
            ("ident", [128, 128]), ("d8", [1, 64]),
            ("selb", [96, 16]), ("selc", [96, 16])]:
        dr[name] = nc.dram_tensor(name, shape, F32, kind="ExternalInput").ap()
    dr["zscr"] = nc.dram_tensor("zscr", [rtiles, ICH, 128, T], F32,
                                kind="Internal").ap()
    dr["yout"] = nc.dram_tensor("yout", [DM, real], F32,
                                kind="ExternalOutput").ap()
    import os as _os
    _tap = _os.environ.get("KTAP", "")
    if _tap:
        dr["dbg"] = nc.dram_tensor("dbg", [128, 16 * T], F32,
                                   kind="ExternalOutput").ap()

    p = B()
    p.sem("sx", "wq1a", "wq1b", "wq2a", "wq2b", "wipa", "wipb",
          "wopa", "wopb", "zoa", "zob", "zia", "zib", "out",
          "te", "ve", "se", "ge")

    es = ExitStack()
    with es:
        sb = {}

        def alloc(name, shape):
            sb[name] = es.enter_context(nc.sbuf_tensor(name, shape, F32))
            return sb[name]

        x1res = alloc("x1res", [128, DCH * real])
        hstate = alloc("hstate", [128, 256])
        atile = alloc("atile", [128, 256])
        xprojw = alloc("xprojw", [128, 16 * 96])
        dtprojw = alloc("dtprojw", [64, 16 * 128])
        wgw = alloc("wgw", [128, 64])
        identt = alloc("identt", [128, 128])
        stats = alloc("stats", [128, 2 * T])
        scorest = alloc("scorest", [128, 2 * T])
        cwt = alloc("cwt", [128, 64])
        cbt = alloc("cbt", [128, 16])
        dvt = alloc("dvt", [128, 16])
        dtbt = alloc("dtbt", [128, 16])
        b1t = alloc("b1t", [128, 128])
        b2mmt = alloc("b2mmt", [8, 1024])
        lngt = alloc("lngt", [128, 8])
        lnbt = alloc("lnbt", [128, 8])
        glubt = alloc("glubt", [128, 16])
        selone = alloc("selone", [8, 128])
        d8t = alloc("d8t", [1, 64])
        onesc = alloc("onesc", [128, 1])
        ones1 = alloc("ones1", [1, 128])
        xdbl = alloc("xdbl", [128, 512])
        carry = alloc("carry", [128, 48])
        selbt = alloc("selbt", [96, 16])
        selct = alloc("selct", [96, 16])
        AR = alloc("arena", [128, 26192])

        def ar(off, size):
            assert off + size <= 26192
            return AR[:, off:off + size]

        def xn_ch(i, c):
            return AR[:, i * 4096 + c * T: i * 4096 + (c + 1) * T]

        X1WF = ar(4096, 4096)
        HT = lambda j: ar(8240 + j * T, T)
        LOGT = AR[0:8, 8240:8240 + T]
        WQ1 = lambda par: ar(16432 + par * 2048, 2048)
        WQ2 = lambda par: ar(20528 + par * 1024, 1024)
        SSRC = ar(22576, T)
        TMP = ar(23088, T)
        TMP2 = ar(23600, T)
        SROW = lambda i: AR[0:1, 24112 + i * T: 24112 + (i + 1) * T]
        RS = lambda i: AR[:, 26160 + i: 26161 + i]

        XPC = lambda c: AR[:, c * 515:(c + 1) * 515]
        DT_ = lambda c: AR[:, c * 515: c * 515 + 512]
        XC = lambda c: ar(8240 + c * T, T)
        X1WM = ar(8240, 4096)
        XN3 = lambda c: ar(16432 + c * T, T)
        WSA = lambda par: ar(20528 + par * 1024, 1024)
        ZST = lambda par: ar(12336 + par * T, T)
        SC = lambda i: ar(16432 + i * T, T)
        YF = lambda c: ar(17968 + c * T, T)
        WSO = lambda par: ar(par * 1024, 1024)
        Y2A = ar(2048, T)
        Y2B = ar(2560, T)
        ZSo = ar(3072, T)
        ZINo = ar(3584, T)
        MACC = lambda m: ar(4096 + m * T, T)
        MT = lambda m: ar(8240 + m * T, T)
        G1 = lambda ob: ar(12336 + ob * T, T)
        SG = lambda ob: ar(16432 + ob * T, T)
        XN2S = ar(20528, T)

        psum = [es.enter_context(nc.psum_tensor(f"pb{i}", [128, T], F32))
                for i in range(8)]

        bankw = [None] * 8

        def bank_wait(eng, b):
            if bankw[b] is not None:
                sem, tok = bankw[b]
                p.w(eng, sem, tok)
                bankw[b] = None

        def mm(lhsT, rhs, bank, start, stop, sig=None, mrows=128):
            return p.do("tensor",
                        lambda e, o=psum[bank][0:mrows, 0:T], l=lhsT, r=rhs,
                        st=start, sp=stop:
                        e.matmul(o, l, r, start=st, stop=sp), sig=sig)

        def recip_safe(dst, srcap, k1, k2):
            vt = p.do("vector", lambda e: e.engine_nop(), sig="ve")
            p.w("scalar", "ve", vt)
            p.do("scalar", lambda e: e.activation(RS(k1), srcap, AF.Square))
            p.do("scalar", lambda e: e.activation(RS(k2), RS(k1), AF.Ln))
            st = p.do("scalar", lambda e: e.activation(
                RS(k1), RS(k2), AF.Exp, scale=-1.0), sig="se")
            p.w("vector", "se", st)
            p.do("vector", lambda e: e.tensor_tensor(
                dst, srcap, RS(k1), ALU.mult))
            p.do("vector", lambda e: e.drain())

        def bcast(row, bank, sig=None):
            return p.do("tensor", lambda e, r=row, b=bank:
                        e.matmul(psum[b][:, 0:T], ones1[:], r,
                                 start=True, stop=True), sig=sig)

        p.do("vector", lambda e: e.memset(AR[:, 0:13096], 0.0))
        for gi in range(30):
            sl = (slice(13096, 26192) if gi % 2 == 0 else slice(0, 13096))
            s2 = (slice(0, 13096) if gi % 2 == 0 else slice(13096, 26192))
            p.do("vector", lambda e, sl=sl, s2=s2: e.tensor_copy(
                AR[:, sl], AR[:, s2]))
        guard = p.do("vector", lambda e: e.engine_nop(), sig="ve")
        p.w("tensor", "ve", guard)
        p.w("scalar", "ve", guard)
        p.w("gpsimd", "ve", guard)
        p.do("vector", lambda e: e.memset(onesc[:], 1.0))
        p.do("vector", lambda e: e.memset(ones1[:], 1.0))
        p.do("vector", lambda e: e.memset(hstate[:], 0.0))
        p.do("vector", lambda e: e.memset(carry[:], 0.0))
        p.do("vector", lambda e: e.drain())

        for name, dst, resh in [
                ("wxp", xprojw, ("k p j -> p k j", "p (k j) -> p k j", 16)),
                ("wdt", dtprojw, ("m p q -> p m q", "p (m q) -> p m q", 16)),
                ("wg", wgw, ("k p j -> p k j", "p (k j) -> p k j", 8))]:
            s_pat, d_pat, kk = resh
            p.dma("sync", "sx", dst[:].rearrange(d_pat, **(
                {"k": kk} if "k" in d_pat.split("->")[1] else {"m": kk})),
                dr[name].rearrange(s_pat))
        for name, dst in [("ident", identt), ("cw", cwt), ("cb", cbt),
                          ("dvec", dvt), ("dtb", dtbt), ("b1", b1t),
                          ("b2mm", b2mmt), ("lng", lngt), ("lnb", lnbt),
                          ("glub", glubt), ("d8", d8t),
                          ("selb", selbt), ("selc", selct)]:
            p.dma("sync", "sx", dst[:], dr[name][:, :])
        p.w("sync", "ve", p.cnt["ve"])
        alog_tok = p.dma("sync", "sx", atile[:], dr["alog"][:, :])
        p.w("scalar", "sx", alog_tok)
        p.do("scalar", lambda e: e.activation(atile[:], atile[:], AF.Exp))
        se_a = p.do("scalar", lambda e: e.mul(atile[:], atile[:], -1.0),
                    sig="se")
        consts_tok = p.cnt["sx"]

        def ln_stats(xch, nch, mu_row, r_row, var_row):
            bank_wait("tensor", 7)
            p.w("scalar", "ve", p.cnt["ve"])
            prev_mm = None
            for c in range(nch):
                if prev_mm is not None:
                    p.w("scalar", "te", prev_mm)
                sq = p.do("scalar", lambda e, c=c: e.activation(
                    TMP, xch(c), AF.Square), sig="se")
                p.do("tensor", lambda e, c=c, n=nch: e.matmul(
                    psum[7][0:1, 0:T], onesc[:], xch(c),
                    start=(c == 0), stop=(c == n - 1)))
                p.w("tensor", "se", sq)
                prev_mm = p.do("tensor", lambda e, c=c, n=nch: e.matmul(
                    psum[7][32:33, 0:T], onesc[:], TMP[:],
                    start=(c == 0), stop=(c == n - 1)), sig="te")
            p.w("vector", "te", prev_mm)
            nf = float(nch * 128)
            p.do("vector", lambda e: e.tensor_scalar_mul(
                mu_row, psum[7][0:1, 0:T], 1.0 / nf))
            p.do("vector", lambda e: e.tensor_scalar_mul(
                var_row, psum[7][32:33, 0:T], 1.0 / nf))
            p.do("vector", lambda e: e.tensor_tensor(
                TMP[0:1, 0:T], mu_row, mu_row, ALU.mult))
            p.do("vector", lambda e: e.drain())
            p.do("vector", lambda e: e.tensor_tensor(
                var_row, var_row, TMP[0:1, 0:T], ALU.subtract))
            p.do("vector", lambda e: e.drain())
            ve_tok = p.do("vector", lambda e: e.tensor_scalar_add(
                TMP[0:1, 0:T], var_row, EPS), sig="ve")
            p.w("scalar", "ve", ve_tok)
            p.do("scalar", lambda e: e.activation(
                TMP2[0:1, 0:T], TMP[0:1, 0:T], AF.Ln))
            se_tok = p.do("scalar", lambda e: e.activation(
                r_row, TMP2[0:1, 0:T], AF.Exp, scale=-0.5), sig="se")
            return ve_tok, se_tok

        def router(xnch, tile_col):
            bank_wait("tensor", 5)
            bank_wait("tensor", 6)
            lt = None
            for k in range(DCH):
                lt = mm(wgw[:, k * 8:(k + 1) * 8], xnch(k), 5,
                        k == 0, k == DCH - 1,
                        sig="te" if k == DCH - 1 else None, mrows=8)
            p.w("vector", "te", lt)
            vt = p.do("vector", lambda e: e.tensor_copy(
                LOGT, psum[5][0:8, 0:T]), sig="ve")
            PT = TMP[:, 0:8]
            EX = TMP[:, 16:24]
            SG1 = TMP[:, 32:40]
            EQ1 = TMP[:, 48:56]
            PM = TMP[:, 64:72]
            SG2 = TMP[:, 80:88]
            EQ2 = TMP[:, 96:104]
            T4 = TMP[:, 112:116]
            T2 = TMP[:, 128:130]
            SC1 = TMP[:, 144:152]
            SC2 = TMP[:, 160:168]
            SCR = TMP[:, 176:184]

            def pv(fn):
                p.do("vector", fn)
                p.do("vector", lambda e: e.drain())

            def tree8(srcfn, dst, op):
                pv(lambda e, s=srcfn, op=op: e.tensor_tensor(
                    T4, s()[:, 0:4], s()[:, 4:8], op))
                pv(lambda e, op=op: e.tensor_tensor(
                    T2, T4[:, 0:2], T4[:, 2:4], op))
                pv(lambda e, d=dst, op=op: e.tensor_tensor(
                    d, T2[:, 0:1], T2[:, 1:2], op))

            ptv = vt
            for tg in range(4):
                sl = slice(tg * 128, (tg + 1) * 128)
                p.w("tensor", "ve", ptv)
                tt = p.do("tensor", lambda e, sl=sl: e.transpose(
                    psum[6][:, 0:8], LOGT[:, sl], identt[0:8, 0:8]),
                    sig="te")
                p.w("vector", "te", tt)
                ptv = p.do("vector", lambda e: e.tensor_copy(
                    PT, psum[6][:, 0:8]), sig="ve")
                tree8(lambda: PT, RS(0), ALU.max)
                v2 = p.do("vector", lambda e: e.tensor_scalar_mul(
                    RS(1), RS(0), -1.0), sig="ve")
                if _tap == "rs1" and tg == 3 and tile_col == T:
                    p.w("gpsimd", "ve", v2)
                    p.dma("gpsimd", "zob", dr["dbg"][0:128, 0:192],
                          TMP[:, 0:192])
                    zt = p.dma("gpsimd", "zob", dr["dbg"][0:128, 192:224],
                               AR[:, 26160:26160 + 32])
                    p.w("vector", "zob", zt)
                p.w("scalar", "ve", v2)
                st = p.do("scalar", lambda e: e.activation(
                    EX, PT, AF.Exp, bias=RS(1)), sig="se")
                p.w("vector", "se", st)
                tree8(lambda: EX, RS(4), ALU.max)
                v2 = p.do("vector", lambda e: e.tensor_scalar_mul(
                    RS(5), RS(4), -1.0), sig="ve")
                p.w("scalar", "ve", v2)
                st = p.do("scalar", lambda e: e.activation(
                    SG1, EX, AF.Sign, bias=RS(5)), sig="se")
                p.w("vector", "se", st)
                pv(lambda e: e.tensor_scalar_add(EQ1, SG1, 1.0))
                tree8(lambda: EQ1, RS(6), ALU.add)
                pv(lambda e: e.scalar_tensor_tensor(
                    PM, EQ1, -2.0, EX, ALU.mult, ALU.add))
                tree8(lambda: PM, RS(7), ALU.max)
                v2 = p.do("vector", lambda e: e.tensor_scalar_mul(
                    RS(8), RS(7), -1.0), sig="ve")
                p.w("scalar", "ve", v2)
                st = p.do("scalar", lambda e: e.activation(
                    SG2, PM, AF.Sign, bias=RS(8)), sig="se")
                p.w("vector", "se", st)
                pv(lambda e: e.tensor_scalar_add(EQ2, SG2, 1.0))
                tree8(lambda: EQ2, RS(9), ALU.add)
                pv(lambda e: e.tensor_scalar_add(RS(10), RS(6), -1.0))
                pv(lambda e: e.tensor_scalar_min(RS(11), RS(10), 1.0))
                pv(lambda e: e.tensor_tensor(
                    RS(26), RS(4), RS(7), ALU.add))
                pv(lambda e: e.tensor_scalar_add(RS(12), RS(26), 1e-15))
                recip_safe(RS(13), RS(12), 22, 23)
                recip_safe(RS(14), RS(6), 22, 23)
                recip_safe(RS(15), RS(9), 22, 23)
                pv(lambda e: e.tensor_scalar(
                    RS(16), RS(11), -1.0, 1.0, ALU.mult, ALU.add))
                pv(lambda e: e.tensor_tensor(RS(17), RS(4), RS(13), ALU.mult))
                pv(lambda e: e.tensor_tensor(
                    RS(18), RS(17), RS(16), ALU.mult))
                pv(lambda e: e.tensor_tensor(
                    RS(19), RS(11), RS(14), ALU.mult))
                pv(lambda e: e.tensor_tensor(
                    RS(2), RS(18), RS(19), ALU.add))
                pv(lambda e: e.tensor_tensor(RS(20), RS(7), RS(13), ALU.mult))
                pv(lambda e: e.tensor_tensor(
                    RS(21), RS(20), RS(16), ALU.mult))
                pv(lambda e: e.tensor_tensor(
                    RS(3), RS(21), RS(15), ALU.mult))
                pv(lambda e: e.tensor_scalar_mul(RS(24), RS(2), 0.5))
                v2 = p.do("vector", lambda e: e.tensor_scalar_mul(
                    RS(25), RS(3), 0.5), sig="ve")
                pv(lambda e: e.tensor_scalar_mul(SC1, EQ1, RS(24)))
                pv(lambda e: e.tensor_scalar_mul(SC2, EQ2, RS(25)))
                v2 = p.do("vector", lambda e: e.tensor_tensor(
                    SCR, SC1, SC2, ALU.add), sig="ve")
                p.w("tensor", "ve", v2)
                bank_wait("tensor", 7)
                tt = p.do("tensor", lambda e: e.transpose(
                    psum[7][0:8, 0:128], SCR, identt[:]), sig="te")
                p.w("vector", "te", tt)
                scv = p.do("vector", lambda e, tg=tg, tc=tile_col:
                           e.tensor_copy(
                               scorest[0:8, tc + tg * 128:
                                       tc + (tg + 1) * 128],
                               psum[7][0:8, 0:128]), sig="ve")
                bankw[7] = ("ve", scv)
            p.w("scalar", "te", tt)
            p.w("vector", "te", tt)
            if _tap == "scores":
                p.w("gpsimd", "ve", p.cnt["ve"])
                p.dma("gpsimd", "zob", dr["dbg"][0:8, 0:2 * T], scorest[0:8, :])
                p.dma("gpsimd", "zob", dr["dbg"][0:128, 2 * T:2 * T + 192],
                      TMP[:, 0:192])
                p.dma("gpsimd", "zob",
                      dr["dbg"][0:128, 2 * T + 192:2 * T + 224],
                      AR[:, 26160:26160 + 32])
            return p.do("vector", lambda e: e.engine_nop(), sig="ve")

        wpar = {"q1": 0, "q2": 0, "ip": 0, "op": 0}
        wpend = {}

        def stream(tag, sems, buf_of, src_of):
            par = wpar[tag]
            sem = sems[par]
            key = (tag, par)
            if key in wpend:
                csem, ctok = wpend[key]
                p.w("sync", csem, ctok)
            tok = p.dma("sync", sem, buf_of(par), src_of())
            wpar[tag] ^= 1
            return par, sem, tok, key

        def fence_sync(gpsimd=False):
            for s in ("te", "ve", "se"):
                p.w("sync", s, p.cnt[s])
                if gpsimd:
                    p.w("gpsimd", s, p.cnt[s])

        def emit_ffn(tiles, x1_of):
            fence_sync()
            routers_only = _os.environ.get("KFFN") == "ln"
            n_experts = int(_os.environ.get("KNE", str(NE)))
            for i, ti in enumerate(tiles):
                xch = x1_of(ti)
                vtk, stk = ln_stats(xch, DCH, SROW(0), SROW(2), SROW(1))
                p.w("tensor", "ve", vtk)
                p.w("tensor", "se", stk)
                bcast(SROW(0), 6)
                brs = bcast(SROW(2), 7, sig="te")
                p.w("vector", "te", brs)
                lastv = None
                for c in range(DCH):
                    p.do("vector", lambda e, i=i, c=c, xch=xch:
                         e.tensor_tensor(xn_ch(i, c), xch(c),
                                         psum[6][:, 0:T], ALU.subtract))
                    lastv = p.do("vector", lambda e, i=i, c=c:
                                 e.tensor_tensor(xn_ch(i, c), xn_ch(i, c),
                                                 psum[7][:, 0:T], ALU.mult),
                                 sig="ve" if c == DCH - 1 else None)
                p.w("tensor", "ve", lastv)
                router(lambda c, i=i: xn_ch(i, c), i * T)
            prev_sr = None
            for e_i in range(0 if routers_only else n_experts):
                bank_wait("tensor", 5)
                bt = p.do("tensor", lambda e, e_i=e_i: e.matmul(
                    psum[5][0:8, 0:128], d8t[0:1, e_i * 8:e_i * 8 + 8],
                    ones1[0:1, 0:128], start=True, stop=True), sig="te")
                p.w("vector", "te", bt)
                if prev_sr is not None:
                    p.w("vector", "te", prev_sr)
                sv = p.do("vector", lambda e: e.tensor_copy(
                    selone[:], psum[5][0:8, 0:128]), sig="ve")
                p.w("tensor", "ve", sv)
                for i, ti in enumerate(tiles):
                    x1c = x1_of(ti)
                    bank_wait("tensor", 5)
                    sr = prev_sr = p.do(
                        "tensor", lambda e, e_i=e_i, i=i: e.matmul(
                            psum[5][:, 0:T],
                            selone[0:8, 0:128],
                            scorest[0:8, i * T:(i + 1) * T],
                            start=True, stop=True), sig="te")
                    p.w("vector", "te", sr)
                    srv = p.do("vector", lambda e: e.tensor_copy(
                        SSRC[:], psum[5][:, 0:T]), sig="ve")
                    bankw[5] = ("ve", srv)
                    if _os.environ.get("KFFN") == "srep":
                        continue
                    for oc in range(8):
                        par, sem, tok, key = stream(
                            "q1", ("wq1a", "wq1b"),
                            lambda par: WQ1(par).rearrange(
                                "p (k j) -> p k j", k=8),
                            lambda e_i=e_i, oc=oc: dr["w1"][
                                e_i, :, :, oc * 256:(oc + 1) * 256]
                            .rearrange("k p j -> p k j"))
                        p.w("tensor", sem, tok)
                        if (_tap == "wq1" and e_i == 0 and i == 0
                                and oc == 0):
                            p.w("gpsimd", sem, tok)
                            dtp = p.dma("gpsimd", "zob", dr["dbg"][:, 0:2048],
                                        WQ1(par))
                            p.w("tensor", "zob", dtp)
                        lastmm = None
                        for jj in range(2):
                            j = oc * 2 + jj
                            bank = j % 4
                            bank_wait("tensor", bank)
                            for k in range(8):
                                lastmm = mm(
                                    WQ1(par)[:, (k * 2 + jj) * 128:
                                             (k * 2 + jj + 1) * 128],
                                    xn_ch(i, k), bank, k == 0, k == 7,
                                    sig="te" if k == 7 else None)
                            p.w("scalar", "te", lastmm)
                            st = p.do("scalar",
                                      lambda e, j=j, b=bank, ei=e_i:
                                      e.activation(
                                          HT(j), psum[b][:, 0:T], AF.Relu,
                                          bias=b1t[:, ei * 16 + j:
                                                   ei * 16 + j + 1]),
                                      sig="se")
                            bankw[bank] = ("se", st)
                        wpend[key] = ("te", lastmm)
                    for b in range(4):
                        bank_wait("tensor", b)
                    if _os.environ.get("KFFN") == "h":
                        continue
                    if _tap == "h00" and e_i == 0 and i == 0:
                        p.w("gpsimd", "se", p.cnt["se"])
                        dtp = p.dma("gpsimd", "zob", dr["dbg"][:, :],
                                    AR[:, 8240:8240 + 16 * T])
                        p.w("tensor", "zob", dtp)
                    p.w("tensor", "ve", srv)
                    lastmm = None
                    for k in range(16):
                        par, sem, tok, key = stream(
                            "q2", ("wq2a", "wq2b"), WQ2,
                            lambda e_i=e_i, k=k: dr["w2"][e_i, k, :, :])
                        p.w("tensor", sem, tok)
                        for m in range(8):
                            lastmm = mm(WQ2(par)[:, m * 128:(m + 1) * 128],
                                        HT(k), m, k == 0, k == 15,
                                        sig="te" if m == 7 else None)
                        wpend[key] = ("te", lastmm)
                    p.w("vector", "te", lastmm)
                    lastv = None
                    for m in range(8):
                        p.do("vector", lambda e, m=m: e.tensor_tensor(
                            TMP[:], psum[m][:, 0:T], SSRC[:], ALU.mult))
                        lastv = p.do("vector",
                                     lambda e, m=m, x1c=x1c:
                                     e.tensor_tensor(x1c(m), x1c(m), TMP[:],
                                                     ALU.add), sig="ve")
                    for m in range(8):
                        bankw[m] = ("ve", lastv)
                    for m in range(8):
                        bank_wait("tensor", m)
            skip_b2 = routers_only or _os.environ.get("KB2") == "0"
            for i, ti in enumerate(tiles) if not skip_b2 else []:
                x1c = x1_of(ti)
                lt = None
                for m in range(8):
                    bank_wait("tensor", m)
                    lt = p.do("tensor", lambda e, m=m, i=i: e.matmul(
                        psum[m][:, 0:T],
                        b2mmt[0:8, m * 128:(m + 1) * 128],
                        scorest[0:8, i * T:(i + 1) * T],
                        start=True, stop=True),
                        sig="te" if m == 7 else None)
                p.w("vector", "te", lt)
                lastv = None
                for m in range(8):
                    lastv = p.do("vector", lambda e, m=m, x1c=x1c:
                                 e.tensor_tensor(x1c(m), x1c(m),
                                                 psum[m][:, 0:T], ALU.add),
                                 sig="ve")
                for m in range(8):
                    bankw[m] = ("ve", lastv)

        _mph = int(_os.environ.get("KMAMPH", "99"))
        wcopy_tok = [None]

        def emit_mamba(ti, is_real):
            rt = ti - wtiles
            if is_real:
                def x1c(c, rt=rt):
                    return x1res[:, c * real + rt * T:
                                 c * real + rt * T + T]
                mu1 = stats[0:1, 0:T]
                r1 = stats[0:1, T:2 * T]
            else:
                p.w("vector", "se", p.cnt["se"])
                lastv = None
                for c in range(DCH):
                    lastv = p.do("vector", lambda e, c=c: e.tensor_copy(
                        X1WM[:, c * T:(c + 1) * T],
                        X1WF[:, c * T:(c + 1) * T]), sig="ve")
                p.w("tensor", "ve", lastv)
                p.w("scalar", "ve", lastv)
                wcopy_tok[0] = lastv

                def x1c(c):
                    return X1WM[:, c * T:(c + 1) * T]
                mu1, r1 = SROW(0), SROW(2)
            vtk, stk = ln_stats(x1c, DCH, mu1, r1, SROW(1))
            p.w("vector", "se", stk)
            p.do("vector", lambda e: e.tensor_tensor(
                SROW(3), r1, r1, ALU.mult))
            p.do("vector", lambda e: e.drain())
            p.do("vector", lambda e: e.tensor_tensor(
                SROW(3), SROW(3), SROW(1), ALU.mult))
            p.do("vector", lambda e: e.drain())
            vt = p.do("vector", lambda e: e.tensor_scalar_add(
                SROW(3), SROW(3), EPS), sig="ve")
            p.w("scalar", "ve", vt)
            p.do("scalar", lambda e: e.activation(SROW(3), SROW(3), AF.Ln))
            st = p.do("scalar", lambda e: e.activation(
                SROW(3), SROW(3), AF.Exp, scale=-0.5), sig="se")
            p.w("vector", "se", st)
            vt = p.do("vector", lambda e: e.tensor_tensor(
                SROW(3), SROW(3), r1, ALU.mult), sig="ve")
            p.w("tensor", "ve", vtk)
            p.w("tensor", "ve", vt)
            bcast(mu1, 6)
            bs3 = bcast(SROW(3), 7, sig="te")
            p.w("vector", "te", bs3)
            lastv = None
            for c in range(DCH):
                p.do("vector", lambda e, c=c, x1c=x1c: e.tensor_tensor(
                    XN3(c), x1c(c), psum[6][:, 0:T], ALU.subtract))
                p.do("vector", lambda e, c=c: e.tensor_tensor(
                    XN3(c), XN3(c), psum[7][:, 0:T], ALU.mult))
                lastv = p.do("vector", lambda e, c=c: e.tensor_scalar(
                    XN3(c), XN3(c), lngt[:, c:c + 1], lnbt[:, c:c + 1],
                    ALU.mult, ALU.add), sig="ve")
            p.w("tensor", "ve", lastv)
            lastc = None
            for c in range(ICH):
                lastc = p.do("vector", lambda e, c=c: e.tensor_copy(
                    AR[:, c * 515:c * 515 + 3],
                    carry[:, c * 3:(c + 1) * 3]), sig="ve")
            p.w("tensor", "ve", lastc)
            if not is_real and _mph < 2:
                return
            fence_sync()
            nm = 32 if is_real else 16
            for m_i in range(nm):
                par, sem, tok, key = stream(
                    "ip", ("wipa", "wipb"),
                    lambda par: WSA(par)[:].rearrange(
                        "p (k q) -> p k q", k=8),
                    lambda m_i=m_i: dr["wip"][m_i].rearrange(
                        "k p q -> p k q"))
                p.w("tensor", sem, tok)
                bank = m_i % 2
                bank_wait("tensor", bank)
                lt = None
                for k in range(8):
                    lt = mm(WSA(par)[:, k * 128:(k + 1) * 128], XN3(k),
                            bank, k == 0, k == 7,
                            sig="te" if k == 7 else None)
                wpend[key] = ("te", lt)
                if m_i < 16:
                    p.w("vector", "te", lt)
                    ev = p.do("vector", lambda e, m_i=m_i, b=bank:
                              e.tensor_copy(
                                  AR[:, m_i * 515 + 3:m_i * 515 + 515],
                                  psum[b][:, 0:T]), sig="ve")
                    bankw[bank] = ("ve", ev)
                else:
                    p.w("vector", "te", lt)
                    zsem = "zoa" if bank == 0 else "zob"
                    zkey = ("zo", bank)
                    if zkey in wpend:
                        csem, ctok = wpend[zkey]
                        p.w("vector", csem, ctok)
                    ev = p.do("vector", lambda e, b=bank: e.tensor_copy(
                        ZST(b)[:], psum[b][:, 0:T]), sig="ve")
                    bankw[bank] = ("ve", ev)
                    p.w("gpsimd", "ve", ev)
                    ztok = p.do("gpsimd",
                                lambda e, m_i=m_i, b=bank, rt=rt:
                                e.dma_start(out=dr["zscr"][rt, m_i - 16, :, :],
                                            in_=ZST(b)[:]),
                                sig=zsem, inc=16)
                    wpend[zkey] = (zsem, ztok)
            bank_wait("vector", 0)
            bank_wait("vector", 1)
            if not is_real and _mph < 3:
                return
            lasts = None
            prev_sil = None
            for c in range(ICH):
                if prev_sil is not None:
                    p.w("vector", "se", prev_sil)
                p.do("vector", lambda e, c=c: e.tensor_scalar_mul(
                    TMP[:], XPC(c)[:, 0:512], cwt[:, c * 4:c * 4 + 1]))
                for k in range(1, 4):
                    p.do("vector", lambda e, c=c, k=k:
                         e.scalar_tensor_tensor(
                             TMP[:], XPC(c)[:, k:k + 512],
                             cwt[:, c * 4 + k:c * 4 + k + 1], TMP[:],
                             ALU.mult, ALU.add))
                vt = p.do("vector", lambda e: e.engine_nop(), sig="ve")
                p.w("scalar", "ve", vt)
                lasts = prev_sil = p.do("scalar", lambda e, c=c: e.activation(
                    XC(c), TMP[:], AF.Silu, bias=cbt[:, c:c + 1]), sig="se")
            for c in range(ICH):
                p.do("vector", lambda e, c=c: e.tensor_copy(
                    carry[:, c * 3:(c + 1) * 3],
                    AR[:, c * 515 + 512:c * 515 + 515]))
                p.do("vector", lambda e: e.drain())
            p.w("tensor", "se", lasts)
            bank_wait("tensor", 2)
            lt = None
            for c in range(ICH):
                lt = p.do("tensor", lambda e, c=c: e.matmul(
                    psum[2][0:96, 0:T], xprojw[:, c * 96:(c + 1) * 96],
                    XC(c), start=(c == 0), stop=(c == ICH - 1)),
                    sig="te" if c == ICH - 1 else None)
            p.w("vector", "te", lt)
            vt = p.do("vector", lambda e: e.tensor_copy(
                xdbl[0:96, :], psum[2][0:96, 0:T]), sig="ve")
            p.w("tensor", "ve", vt)
            p.w("scalar", "ve", vt)
            p.w("gpsimd", "ve", vt)
            bankw[2] = None
            if not is_real and _mph < 4:
                return
            lastd = None
            for m_i in range(ICH):
                bank = m_i % 2
                bank_wait("tensor", bank)
                lt = p.do("tensor", lambda e, m_i=m_i, b=bank: e.matmul(
                    psum[b][:, 0:T],
                    dtprojw[:, m_i * 128:(m_i + 1) * 128],
                    xdbl[0:64, :], start=True, stop=True), sig="te")
                p.w("scalar", "te", lt)
                p.do("scalar", lambda e, m_i=m_i, b=bank: e.activation(
                    TMP2[:], psum[b][:, 0:T], AF.Exp,
                    bias=dtbt[:, m_i:m_i + 1]))
                lastd = p.do("scalar", lambda e, m_i=m_i: e.activation(
                    DT_(m_i), TMP2[:], AF.Ln, bias=1.0), sig="se")
                bankw[bank] = ("se", lastd)
            p.w("vector", "se", lastd)
            p.w("tensor", "se", lastd)
            bankw[0] = bankw[1] = None
            if not is_real and _mph < 5:
                return
            ROWB = SC(0)[0:1, 0:T]
            ROWC = SC(1)[0:1, 0:T]
            prev_read = None
            for n in range(NS):
                bank_wait("tensor", 2)
                xb = p.do("tensor", lambda e, n=n: e.matmul(
                    psum[2][0:1, 0:T], selbt[0:96, n:n + 1], xdbl[0:96, :],
                    start=True, stop=True), sig="te")
                if is_real:
                    bank_wait("tensor", 3)
                    xb = p.do("tensor", lambda e, n=n: e.matmul(
                        psum[3][0:1, 0:T], selct[0:96, n:n + 1],
                        xdbl[0:96, :], start=True, stop=True), sig="te")
                p.w("vector", "te", xb)
                rv = p.do("vector", lambda e: e.tensor_copy(
                    ROWB, psum[2][0:1, 0:T]), sig="ve")
                bankw[2] = ("ve", rv)
                if is_real:
                    rv = p.do("vector", lambda e: e.tensor_copy(
                        ROWC, psum[3][0:1, 0:T]), sig="ve")
                    bankw[3] = ("ve", rv)
                if prev_read is not None:
                    p.w("tensor", "ve", prev_read)
                p.w("tensor", "ve", rv)
                bb = bcast(ROWB, 0, sig="te")
                if is_real:
                    bb = bcast(ROWC, 1, sig="te")
                p.w("vector", "te", bb)
                prev_st = None
                for c in range(ICH):
                    p.do("vector", lambda e, c=c: e.tensor_tensor(
                        SC(1), DT_(c), XC(c), ALU.mult))
                    p.do("vector", lambda e: e.tensor_tensor(
                        SC(1), SC(1), psum[0][:, 0:T], ALU.mult))
                    p.do("vector", lambda e, c=c, n=n: e.tensor_scalar_mul(
                        SC(0), DT_(c),
                        atile[:, c * 16 + n:c * 16 + n + 1]))
                    vt = p.do("vector", lambda e: e.engine_nop(), sig="ve")
                    p.w("scalar", "ve", vt)
                    st = p.do("scalar", lambda e: e.activation(
                        SC(2), SC(0), AF.Exp), sig="se")
                    p.w("vector", "se", st)
                    p.do("vector", lambda e, c=c, n=n: e.tensor_tensor_scan(
                        SC(0), SC(2), SC(1),
                        hstate[:, c * 16 + n:c * 16 + n + 1],
                        ALU.mult, ALU.add))
                    p.do("vector", lambda e: e.drain())
                    last = p.do("vector", lambda e, c=c, n=n: e.tensor_copy(
                        hstate[:, c * 16 + n:c * 16 + n + 1],
                        SC(0)[:, 511:512]),
                        sig="ve" if (not is_real and c == ICH - 1) else None)
                    if is_real:
                        if n == 0:
                            last = p.do("vector", lambda e, c=c:
                                        e.tensor_tensor(
                                            YF(c), SC(0), psum[1][:, 0:T],
                                            ALU.mult),
                                        sig="ve" if c == ICH - 1 else None)
                        else:
                            p.do("vector", lambda e: e.tensor_tensor(
                                SC(2), SC(0), psum[1][:, 0:T], ALU.mult))
                            last = p.do("vector", lambda e, c=c:
                                        e.tensor_tensor(
                                            YF(c), YF(c), SC(2), ALU.add),
                                        sig="ve" if c == ICH - 1 else None)
                prev_read = last
            if not is_real:
                return
            fence_sync(gpsimd=True)
            prev_y2mm = None
            prev_sil = None
            zkey = ("zi", 0)
            for c in range(ICH):
                if zkey in wpend:
                    csem, ctok = wpend[zkey]
                    p.w("gpsimd", csem, ctok)
                ztok = p.do("gpsimd", lambda e, c=c, rt=rt:
                            e.dma_start(out=ZINo[:],
                                        in_=dr["zscr"][rt, c, :, :]),
                            sig="zia", inc=16)
                p.w("scalar", "zia", ztok)
                if prev_sil is not None:
                    p.w("scalar", "ve", prev_sil)
                st = p.do("scalar", lambda e: e.activation(
                    ZSo[:], ZINo[:], AF.Silu), sig="se")
                wpend[zkey] = ("se", st)
                if prev_y2mm is not None:
                    p.w("vector", "te", prev_y2mm)
                p.do("vector", lambda e, c=c: e.scalar_tensor_tensor(
                    Y2A[:], XC(c), dvt[:, c:c + 1], YF(c),
                    ALU.mult, ALU.add))
                p.w("vector", "se", st)
                yv = prev_sil = p.do("vector", lambda e: e.tensor_tensor(
                    Y2B[:], Y2A[:], ZSo[:], ALU.mult), sig="ve")
                par, sem, tok, key = stream(
                    "op", ("wopa", "wopb"), WSO,
                    lambda c=c: dr["wop"][c, :, :])
                p.w("tensor", sem, tok)
                p.w("tensor", "ve", yv)
                lastev = None
                for m in range(8):
                    bank = 2 + (m % 6)
                    bank_wait("tensor", bank)
                    prev_y2mm = p.do(
                        "tensor", lambda e, m=m, b=bank, par=par: e.matmul(
                            psum[b][:, 0:T],
                            WSO(par)[:, m * 128:(m + 1) * 128],
                            Y2B[:], start=True, stop=True), sig="te")
                    p.w("vector", "te", prev_y2mm)
                    if c == 0:
                        lastev = p.do("vector", lambda e, m=m, b=bank:
                                      e.tensor_copy(MACC(m),
                                                    psum[b][:, 0:T]),
                                      sig="ve")
                    else:
                        lastev = p.do("vector", lambda e, m=m, b=bank:
                                      e.tensor_tensor(MACC(m), MACC(m),
                                                      psum[b][:, 0:T],
                                                      ALU.add), sig="ve")
                    bankw[bank] = ("ve", lastev)
                wpend[key] = ("te", prev_y2mm)
            p.w("scalar", "ve", lastev)
            lasts = None
            for m in range(8):
                lasts = p.do("scalar", lambda e, m=m: e.activation(
                    MT(m), MACC(m), AF.Gelu), sig="se")
            p.w("tensor", "se", lasts)
            sig_tok = None
            for half in range(2):
                gl = None
                for k in range(8):
                    par, sem, tok, key = stream(
                        "op", ("wopa", "wopb"), WSO,
                        lambda k=k, half=half: dr["wglu"][
                            k, :, half * 1024:(half + 1) * 1024])
                    p.w("tensor", sem, tok)
                    if k == 0:
                        for b in range(8):
                            bank_wait("tensor", b)
                    for ob in range(8):
                        gl = mm(WSO(par)[:, ob * 128:(ob + 1) * 128],
                                MT(k), ob, k == 0, k == 7,
                                sig="te" if ob == 7 else None)
                    wpend[key] = ("te", gl)
                if half == 0:
                    p.w("vector", "te", gl)
                    lastv = None
                    for ob in range(8):
                        lastv = p.do("vector", lambda e, ob=ob:
                                     e.tensor_scalar_add(
                                         G1(ob), psum[ob][:, 0:T],
                                         glubt[:, ob:ob + 1]), sig="ve")
                    for ob in range(8):
                        bankw[ob] = ("ve", lastv)
                else:
                    p.w("scalar", "te", gl)
                    lasts = None
                    for ob in range(8):
                        lasts = p.do("scalar", lambda e, ob=ob:
                                     e.activation(SG(ob), psum[ob][:, 0:T],
                                                  AF.Sigmoid,
                                                  bias=glubt[:, 8 + ob:
                                                             9 + ob]),
                                     sig="se")
                    sig_tok = lasts
                    for ob in range(8):
                        bankw[ob] = ("se", lasts)
            p.w("vector", "se", sig_tok)
            bank_wait("tensor", 6)
            bank_wait("tensor", 7)
            bcast(mu1, 6)
            br1 = bcast(r1, 7, sig="te")
            p.w("vector", "te", br1)
            lastv = None
            for c in range(DCH):
                p.do("vector", lambda e, c=c: e.tensor_tensor(
                    TMP[:], G1(c), SG(c), ALU.mult))
                p.do("vector", lambda e, c=c, x1c=x1c: e.tensor_tensor(
                    XN2S[:], x1c(c), psum[6][:, 0:T], ALU.subtract))
                p.do("vector", lambda e: e.tensor_tensor(
                    XN2S[:], XN2S[:], psum[7][:, 0:T], ALU.mult))
                p.do("vector", lambda e: e.tensor_tensor(
                    TMP[:], TMP[:], XN2S[:], ALU.add))
                lastv = p.do("vector", lambda e, c=c, x1c=x1c:
                             e.tensor_tensor(x1c(c), x1c(c), TMP[:],
                                             ALU.add), sig="ve")
            bankw[6] = ("ve", lastv)
            bankw[7] = ("ve", lastv)

        p.w("vector", "sx", consts_tok)
        p.w("tensor", "sx", consts_tok)
        p.w("tensor", "se", se_a)
        p.w("vector", "se", se_a)
        p.w("gpsimd", "sx", consts_tok)

        def load_x1(ti, is_real):
            rt = ti - wtiles
            if is_real:
                dst = x1res[:].rearrange("p (c s) -> p c s", c=DCH)[
                    :, :, rt * T:(rt + 1) * T]
            else:
                dst = X1WF[:].rearrange("p (c s) -> p c s", c=DCH)
            src = dr["xs"].rearrange("(c p) s -> p c s", p=128)[
                :, :, ti * T:(ti + 1) * T]
            return p.dma("sync", "sx", dst, src)

        def x1w_of(ti):
            return lambda c: X1WF[:, c * T:(c + 1) * T]

        def x1r_of(ti, rt=None):
            rt = ti - wtiles
            return lambda c, rt=rt: x1res[:, c * real + rt * T:
                                          c * real + rt * T + T]

        do_mamba = stage in ("mamba", "full")
        do_mamba_w = stage in ("mamba0", "mamba", "full")
        for ti in range(wtiles):
            if ti > 0:
                for s in ("te", "ve", "se"):
                    p.w("sync", s, p.cnt[s])
            xt = load_x1(ti, False)
            for eng in ("vector", "tensor", "scalar"):
                p.w(eng, "sx", xt)
            emit_ffn([ti], x1w_of)
            if do_mamba_w:
                emit_mamba(ti, is_real=False)
        for reg in range(rtiles // 2):
            t0, t1 = wtiles + 2 * reg, wtiles + 2 * reg + 1
            xt = None
            for ti in (t0, t1):
                xt = load_x1(ti, True)
            for eng in ("vector", "tensor", "scalar"):
                p.w(eng, "sx", xt)
            emit_ffn([t0, t1], x1r_of)
            if do_mamba:
                emit_mamba(t0, is_real=True)
                emit_mamba(t1, is_real=True)
        if stage == "full":
            for reg in range(rtiles // 2):
                t0, t1 = wtiles + 2 * reg, wtiles + 2 * reg + 1
                emit_ffn([t0, t1], x1r_of)
        vfin = p.do("vector", lambda e: e.engine_nop(), sig="ve")
        p.w("gpsimd", "ve", vfin)
        otok = p.do("gpsimd", lambda e: e.dma_start(
            out=dr["yout"].rearrange("(c p) s -> p c s", p=128),
            in_=x1res[:].rearrange("p (c s) -> p c s", c=DCH)),
            sig="out", inc=16)
        p.w("gpsimd", "out", otok)

        sems = {}
        for name in p.cnt:
            sems[name] = es.enter_context(nc.semaphore(name))
        block = es.enter_context(nc.Block())
        for engname in B.ENGS:
            def run(eng, q=p.q[engname]):
                for f in q:
                    f(eng, sems)
            getattr(block, engname)(run)

    return nc, p.cnt


_WTILES, _RTILES = 3, 4


def kernel(**inputs):
    w = _stage_weights(inputs)
    spans = _stage_x(inputs["L0_attn_x"], _WTILES, _RTILES)
    nc, _ = build_program(_WTILES, _RTILES)
    in_maps = []
    for core in range(8):
        m = dict(w)
        m["xs"] = spans[core]
        in_maps.append(m)
    res = run_bass_kernel_spmd(nc, in_maps, core_ids=list(range(8)))
    out = np.empty((2, 8192, 1024), np.float32)
    for core in range(8):
        seq, s0 = core // 4, (core % 4) * 2048
        out[seq, s0:s0 + 2048, :] = res.results[core]["yout"].T
    return out
